# version 08
# baseline (speedup 1.0000x reference)
import math
from collections import deque
from contextlib import ExitStack

import numpy as np
import ml_dtypes

import concourse.bass as bass
import concourse.mybir as mybir
from concourse.bass_utils import run_bass_kernel_spmd

F32 = mybir.dt.float32
BF16 = mybir.dt.bfloat16
I32 = mybir.dt.int32
AF = mybir.ActivationFunctionType
OP = mybir.AluOpType

SEMS = {}


class Cfg:
    def __init__(self, dm, l, din, nst, rdt, inter, ncores, cl, cb):
        self.DM, self.L, self.DIN, self.NST, self.RDT = dm, l, din, nst, rdt
        self.INTER, self.NC = inter, ncores
        self.CL, self.CB = cl, cb
        self.KC = 4
        self.NKD = dm // 128
        self.NJ = din // 128
        self.NOT = inter // 128
        self.NMT = dm // 128
        self.NLC = l // cl
        self.NCB = l // cb
        self.OTP = math.ceil(self.NOT / ncores)
        self.DMS = dm // ncores
        self.EDBL = rdt + 2 * nst
        self.EPS = 1e-5
        self.CBM = max(cb // 2, 64)
        self.CBLN = min(cb, 256)
        assert inter % 128 == 0 and self.DMS % 128 == 0 and cl % cb == 0
        assert cl % self.CBM == 0 and l % self.CBLN == 0


FULL = Cfg(dm=2048, l=2048, din=4096, nst=16, rdt=128, inter=5504, ncores=8,
           cl=1024, cb=512)
SMALL = Cfg(dm=256, l=512, din=512, nst=16, rdt=16, inter=256, ncores=2,
            cl=256, cb=128)


class Plan:
    ENGINES = ("sync", "vector", "scalar", "tensor", "gpsimd")

    def __init__(self):
        self.ops = {e: [] for e in self.ENGINES}
        self.emitted = {}
        self.scopes = {e: [] for e in self.ENGINES}
        self.cur_scope = "init"

    def set_scope(self, name):
        self.cur_scope = name

    def wait(self, eng, sem, val):
        if val is None or val <= 0:
            return
        assert val <= self.emitted.get(sem, 0), \
            f"plan deadlock: {eng} waits {sem}>={val} emitted={self.emitted.get(sem, 0)}"
        self.ops[eng].append(lambda e, s=sem, v=val: e.wait_ge(SEMS[s], v))
        self.scopes[eng].append(self.cur_scope)

    def op(self, eng, fn, inc=None, inc_n=1):
        self.scopes[eng].append(self.cur_scope)
        if inc is None:
            self.ops[eng].append(fn)
            return None
        self.emitted[inc] = self.emitted.get(inc, 0) + inc_n
        val = self.emitted[inc]
        self.ops[eng].append(
            lambda e, f=fn, s=inc, n=inc_n: f(e).then_inc(SEMS[s], n))
        return val

    def dma(self, eng, out, in_, q):
        return self.op(eng, lambda e, o=out, i=in_: e.dma_start(out=o, in_=i),
                       inc=q, inc_n=16)


class Arena:
    def __init__(self, tensor, nbytes, esize):
        self.t, self.nbytes, self.esize = tensor, nbytes, esize
        self.off = 0
        self.marks = []
        self.peak = 0

    def alloc(self, parts, elems):
        off = (self.off + self.esize * 16 - 1) // (self.esize * 16) * (self.esize * 16)
        self.off = off + elems * self.esize
        self.peak = max(self.peak, self.off)
        assert self.off <= self.nbytes, f"arena overflow {self.off}>{self.nbytes}"
        return self.t[0:parts, off // self.esize:off // self.esize + elems]

    def mark(self):
        self.marks.append(self.off)

    def release(self):
        self.off = self.marks.pop()


def build(cfg: Cfg, debug=False, DBG_TAPS=False, DBG_T=0):
    C = cfg
    NST, RDT, KC, EDBL = C.NST, C.RDT, C.KC, C.EDBL
    nc = bass.Bass("TRN2", target_bir_lowering=False)

    def ein(name, shape, dt):
        return nc.dram_tensor(name, list(shape), dt, kind="ExternalInput").ap()

    hT = ein("hT", (C.DM, C.L), F32)
    winT = ein("winT", (C.DM, 2 * C.DIN), F32)
    wxT = ein("wxT", (C.DIN, EDBL), F32)
    wdtT = ein("wdtT", (RDT, C.DIN), F32)
    wosl = ein("wosl", (C.DIN, C.DMS), F32)
    convw = ein("convw", (128, C.NJ * KC), F32)
    convb = ein("convb", (128, C.NJ), F32)
    dtb = ein("dtb", (128, C.NJ), F32)
    alog = ein("alog", (128, C.NJ * NST), F32)
    dvec = ein("dvec", (128, C.NJ), F32)
    lutv = ein("lutv", (1, 16), F32)
    gidx = ein("gidx", (C.DM, C.OTP * 128), I32)
    uidx = ein("uidx", (C.DM, C.OTP * 128), I32)
    didx = ein("didx", (C.INTER, C.DMS), I32)
    Ht = ein("Ht", (C.NJ, C.NJ, 128, 128), BF16)
    I128d = ein("I128", (128, 128), BF16)
    ones128 = ein("ones128", (128, 1), F32)

    outT = nc.dram_tensor("outT", [C.DM, C.L], F32, kind="ExternalOutput").ap()

    xnT_d = nc.dram_tensor("xnT_d", [C.DM, C.L], BF16).ap()
    u_d = nc.dram_tensor("u_d", [C.DIN, C.L], BF16).ap()
    sz_d = nc.dram_tensor("sz_d", [C.DIN, C.L], BF16).ap()
    bc_d = nc.dram_tensor("bc_d", [NST, 2 * C.L], BF16).ap()
    ln_d = nc.dram_tensor("ln_d", [2, C.L], F32).ap()
    y_d = nc.dram_tensor("y_d", [C.DIN, C.L], BF16).ap()
    gT_d = nc.dram_tensor("gT_d", [C.DM, C.L], BF16).ap()
    nmt_l = C.DMS // 128
    wg_sl = nc.dram_tensor("wg_sl", [C.OTP * C.NKD, 128, 128], BF16)
    wu_sl = nc.dram_tensor("wu_sl", [C.OTP * C.NKD, 128, 128], BF16)
    wd_sl = nc.dram_tensor("wd_sl", [nmt_l * C.NOT, 128, 128], BF16)
    w2_sl = nc.dram_tensor("w2_sl", [C.NJ, 128, C.DMS], BF16)
    wg_d = nc.dram_tensor("wg_d", [C.NC * C.OTP * C.NKD, 128, 128], BF16).ap()
    wu_d = nc.dram_tensor("wu_d", [C.NC * C.OTP * C.NKD, 128, 128], BF16).ap()
    wd_d = nc.dram_tensor("wd_d", [C.NC * nmt_l * C.NOT, 128, 128], BF16).ap()
    w2_d = nc.dram_tensor("w2_d", [C.NC * C.NJ, 128, C.DMS], BF16).ap()
    scd_d = nc.dram_tensor("scd_d", [2, 128, C.CL], F32).ap()
    car_d = nc.dram_tensor("car_d", [128, C.NJ * C.NST], F32).ap()
    scdB_d = nc.dram_tensor("scdB_d", [3, 128, C.CL], BF16).ap()

    pl = Plan()
    es = ExitStack()

    AFB = 40 * 1024
    ABB = 147 * 1024
    arenaF_t = es.enter_context(nc.sbuf_tensor("arenaF", [128, AFB // 4], F32))
    arenaB_t = es.enter_context(nc.sbuf_tensor("arenaB", [128, ABB // 2], BF16))
    arF = Arena(arenaF_t, AFB, 4)
    arB = Arena(arenaB_t, ABB, 2)

    def g_sb(name, shape, dt):
        return es.enter_context(nc.sbuf_tensor(name, list(shape), dt))

    i128 = g_sb("i128s", (128, 128), BF16)
    ones_s = g_sb("ones_s", (128, 1), F32)
    lutbc = g_sb("lutbc", (128, 16), F32)
    convw_s = g_sb("convw_s", (128, C.NJ * KC), F32)
    convb_s = g_sb("convb_s", (128, C.NJ), F32)
    dtb_s = g_sb("dtb_s", (128, C.NJ), F32)
    dvec_s = g_sb("dvec_s", (128, C.NJ), F32)
    negA = g_sb("negA", (128, C.NJ * NST), F32)
    wdt_bf = g_sb("wdt_bf", (RDT, C.DIN), BF16)
    dtr_bf = g_sb("dtr_bf", (RDT, C.L), BF16)
    carry = g_sb("carry", (128, C.NJ * NST), F32)
    diagD = g_sb("diagD", (128, 128), BF16)
    idx_i = g_sb("idx_i", (128, max(C.OTP * 128, C.DMS)), I32)

    pq = [es.enter_context(nc.psum_tensor(f"pq{i}", [128, 1024], F32))
          for i in range(4)]

    qs, qa, qb, qc2, eT, eV, eS, cc = "qs", "qa", "qb", "qc2", "eT", "eV", "eS", "cc"
    qy = "qy"
    cg = "cg"

    pl.set_scope("ph0")
    pl.dma("sync", i128[:], I128d[:], qs)
    pl.dma("sync", ones_s[:], ones128[:], qs)
    pl.dma("sync", lutbc[:], lutv[0:1, :].partition_broadcast(128), qs)
    pl.dma("sync", convw_s[:], convw[:], qs)
    pl.dma("sync", convb_s[:], convb[:], qs)
    pl.dma("sync", dtb_s[:], dtb[:], qs)
    pl.dma("sync", dvec_s[:], dvec[:], qs)
    v_consts = pl.dma("sync", negA[:], alog[:], qs)

    arF.mark()
    tmpA = arF.alloc(128, C.NJ * NST)
    wdt_f = arF.alloc(RDT, C.DIN)
    pl.wait("scalar", qs, v_consts)
    v_t = pl.op("scalar", lambda e: e.activation(out=tmpA, in_=negA[:], func=AF.Exp),
                inc=eS)
    pl.wait("vector", eS, v_t)
    v_negA = pl.op("vector", lambda e: e.tensor_scalar_mul(
        out=negA[:], in0=tmpA, scalar1=-1.0), inc=eV)
    v1 = pl.dma("sync", wdt_f, wdtT[:], qs)
    pl.wait("scalar", qs, v1)
    v_wdtc = pl.op("scalar", lambda e: e.copy(out=wdt_bf[:], in_=wdt_f), inc=eS)
    arF.release()

    arB.mark()
    matw = max(C.OTP * 128, C.DMS)
    idx_b = [arB.alloc(128, matw) for _ in range(2)]
    acc_bb = [arB.alloc(128, matw) for _ in range(2)]
    tmp_b = arB.alloc(128, matw)
    st = {"acc_store": {0: 0, 1: 0}, "idx_used": {0: 0, 1: 0}, "kt0": 0}

    def materialize(idx_dram, nrt, width, store_fn):
        for kt in range(nrt):
            gp = (st["kt0"] + kt) % 2
            pl.wait("sync", eV, max(st["idx_used"].values()))
            vq = pl.dma("sync", idx_i[0:128, 0:width],
                        idx_dram[kt * 128:(kt + 1) * 128, :], qa)
            pl.wait("vector", qa, vq)
            pl.wait("vector", qy, st["acc_store"][gp])
            pl.wait("vector", qs, v_consts)
            acc_b = acc_bb[gp]
            st["idx_used"][gp] = pl.op("vector", lambda e, w=width, gp=gp: e.tensor_copy(
                out=idx_b[gp][:, 0:w], in_=idx_i[0:128, 0:w]),
                inc=eV)
            pl.op("vector", lambda e, w=width, gp=gp, acc_b=acc_b: e.tensor_scalar(
                out=acc_b[:, 0:w], in0=idx_b[gp][:, 0:w], scalar1=0.0,
                scalar2=lutbc[:, 0:1], op0=OP.is_equal, op1=OP.mult))
            v = None
            for k in range(1, 16):
                pl.op("vector", lambda e, k=k, w=width, gp=gp: e.tensor_scalar(
                    out=tmp_b[:, 0:w], in0=idx_b[gp][:, 0:w], scalar1=float(k),
                    scalar2=lutbc[:, k:k + 1], op0=OP.is_equal, op1=OP.mult))
                v = pl.op("vector", lambda e, w=width, acc_b=acc_b: e.tensor_add(
                    out=acc_b[:, 0:w], in0=acc_b[:, 0:w], in1=tmp_b[:, 0:w]),
                    inc=eV if k == 15 else None)
            pl.wait("scalar", eV, v)
            last = 0
            for o_ap, i_ap in store_fn(kt, acc_b):
                last = pl.dma("scalar", o_ap, i_ap, qy)
            st["acc_store"][gp] = last
        st["kt0"] += nrt

    def gu_store(sl):
        def f(kt, acc_b):
            view = sl.ap().rearrange("(o k) p e -> o k p e", k=C.NKD)
            return [(view[o, kt, :, :], acc_b[:, o * 128:(o + 1) * 128])
                    for o in range(C.OTP)]
        return f

    def dn_store(kt, acc_b):
        view = wd_sl.ap().rearrange("(m k) p e -> m k p e", k=C.NOT)
        return [(view[m, kt, :, :], acc_b[:, m * 128:(m + 1) * 128])
                for m in range(nmt_l)]

    pl.set_scope("mat")
    materialize(gidx, C.NKD, C.OTP * 128, gu_store(wg_sl))
    materialize(uidx, C.NKD, C.OTP * 128, gu_store(wu_sl))
    materialize(didx, C.NOT, C.DMS, dn_store)
    v_mat_stores = pl.emitted.get(qy, 0)
    arB.release()

    pl.set_scope("w2")
    arB.mark()
    arF.mark()
    wo_b = arB.alloc(128, C.NJ * C.DMS)
    w2ev = arB.alloc(128, C.DMS)
    hsl = [arB.alloc(128, C.NJ * 128) for _ in range(2)]
    wo_f = arF.alloc(128, C.NJ * C.DMS)
    pl.wait("sync", eS, v_wdtc)
    pl.wait("sync", eV, v_negA)
    vq = pl.dma("sync", wo_f.rearrange("p (k e) -> p k e", e=C.DMS),
                wosl.rearrange("(k p) e -> p k e", p=128)[:], qs)
    pl.wait("vector", qs, vq)
    pl.wait("vector", qy, v_mat_stores)
    v_wocvt = pl.op("vector", lambda e: e.tensor_copy(out=wo_b, in_=wo_f), inc=eV)
    NSL = (C.DMS + 511) // 512
    v_w2ev, v_w2st = 0, 0
    vW2mm = {}
    for m in range(C.NJ):
        buf = hsl[m % 2]
        if m >= 2:
            pl.wait("sync", qy, v_w2st)
        vq = pl.dma("sync", buf.rearrange("p (k e) -> p k e", e=128),
                    Ht[m, :, :, :].rearrange("k p e -> p k e"), qs)
        pl.wait("tensor", qs, vq)
        pl.wait("tensor", eV, v_wocvt)
        pl.wait("tensor", eV, v_w2ev)
        vmm = None
        for kt in range(C.NJ):
            for si in range(NSL):
                s_, n_ = si * 512, min(512, C.DMS - si * 512)
                pw = pq[2][:, 0:512] if si == 0 else pq[2][:, 512:1024]
                vmm = pl.op("tensor", lambda e, b=buf, kt=kt, s_=s_, n_=n_, pw=pw: e.matmul(
                    out=pw[:, 0:n_], lhsT=b[:, kt * 128:(kt + 1) * 128],
                    rhs=wo_b[:, kt * C.DMS + s_:kt * C.DMS + s_ + n_],
                    start=(kt == 0), stop=(kt == C.NJ - 1)),
                    inc=eT if (kt == C.NJ - 1 and si == NSL - 1) else None)
        vW2mm[m] = vmm
        pl.wait("vector", eT, vmm)
        pl.wait("vector", qy, v_w2st)
        for si in range(NSL):
            s_, n_ = si * 512, min(512, C.DMS - si * 512)
            pw = pq[2][:, 0:512] if si == 0 else pq[2][:, 512:1024]
            v_w2ev = pl.op("vector", lambda e, s_=s_, n_=n_, pw=pw: e.tensor_copy(
                out=w2ev[:, s_:s_ + n_], in_=pw[:, 0:n_]),
                inc=eV if si == NSL - 1 else None)
        pl.wait("scalar", eV, v_w2ev)
        v_w2st = pl.dma("scalar", w2_sl.ap()[m, :, :], w2ev, qy)
    v_w2_stores = pl.emitted.get(qy, 0)
    arB.release()
    arF.release()

    pl.wait("gpsimd", qy, v_mat_stores)
    for sl, full in ((wg_sl, wg_d), (wu_sl, wu_d), (wd_sl, wd_d)):
        pl.op("gpsimd", lambda e, s=sl, f=full: e.collective_compute(
            "AllGather", OP.bypass, [list(range(C.NC))],
            [s.ap().opt()], [f.opt()]), inc=cc)
    pl.wait("gpsimd", qy, v_w2_stores)
    v_cc4 = pl.op("gpsimd", lambda e: e.collective_compute(
        "AllGather", OP.bypass, [list(range(C.NC))],
        [w2_sl.ap().opt()], [w2_d.opt()]), inc=cc)

    pl.set_scope("ln")
    arB.mark()
    xnT = arB.alloc(128, C.NKD * C.L)
    arF.mark()
    CBN = C.CBLN
    NCBN = C.L // CBN
    hTc = arF.alloc(128, C.NKD * CBN)
    sq2 = [arF.alloc(128, CBN) for _ in range(2)]
    r_mean = arF.alloc(1, CBN)
    r_t = arF.alloc(1, CBN)
    r_var = arF.alloc(1, CBN)
    r_rstd = arF.alloc(1, CBN)
    r_ms = arF.alloc(1, CBN)
    bc_rstd = arF.alloc(128, CBN)
    bc_ms = arF.alloc(128, CBN)
    pln0 = pq[3][0:1, 0:512]
    pln1 = pq[3][0:1, 512:1024]
    v_apply, v_rows = 0, 0
    vstat_hist = {}
    for cb in range(NCBN):
        lo = cb * CBN
        if cb > 0:
            pl.wait("sync", eV, v_apply)
        else:
            pl.wait("sync", eV, v_negA)
            pl.wait("sync", eV, v_wocvt)
            pl.wait("sync", eS, v_wdtc)
        vq = pl.dma("sync", hTc.rearrange("p (k l) -> p k l", l=CBN),
                    hT.rearrange("(k p) l -> p k l", p=128)[:, :, lo:lo + CBN], qs)
        pl.wait("tensor", qs, vq)
        pl.wait("tensor", eV, v_rows)
        for kt in range(C.NKD):
            pl.op("tensor", lambda e, kt=kt: e.matmul(
                out=pln0[:, 0:CBN], lhsT=ones_s[:],
                rhs=hTc[:, kt * CBN:(kt + 1) * CBN],
                start=(kt == 0), stop=(kt == C.NKD - 1)))
        pl.wait("scalar", qs, vq)
        vstat = None
        for kt in range(C.NKD):
            pl.wait("scalar", eT, vstat_hist.get((cb, kt - 2), 0))
            vsq = pl.op("scalar", lambda e, kt=kt: e.activation(
                out=sq2[kt % 2], in_=hTc[:, kt * CBN:(kt + 1) * CBN],
                func=AF.Square), inc=eS)
            pl.wait("tensor", eS, vsq)
            vstat = pl.op("tensor", lambda e, kt=kt: e.matmul(
                out=pln1[:, 0:CBN], lhsT=ones_s[:], rhs=sq2[kt % 2],
                start=(kt == 0), stop=(kt == C.NKD - 1)), inc=eT)
            vstat_hist[(cb, kt)] = vstat
        pl.wait("vector", eT, vstat)
        pl.op("vector", lambda e: e.tensor_scalar_mul(
            out=r_mean, in0=pln0[:, 0:CBN], scalar1=1.0 / C.DM))
        pl.op("vector", lambda e: e.tensor_scalar_mul(
            out=r_t, in0=pln1[:, 0:CBN], scalar1=1.0 / C.DM))
        pl.op("vector", lambda e: e.tensor_mul(out=r_var, in0=r_mean, in1=r_mean))
        pl.op("vector", lambda e: e.tensor_sub(out=r_var, in0=r_t, in1=r_var))
        v_rows = pl.op("vector", lambda e: e.tensor_scalar_add(
            out=r_var, in0=r_var, scalar1=C.EPS), inc=eV)
        pl.wait("scalar", eV, v_rows)
        v_ln = pl.op("scalar", lambda e: e.activation(out=r_rstd, in_=r_var,
                                                      func=AF.Ln), inc=eS)
        pl.wait("vector", eS, v_ln)
        v_half = pl.op("vector", lambda e: e.tensor_scalar_mul(
            out=r_rstd, in0=r_rstd, scalar1=-0.5), inc=eV)
        pl.wait("scalar", eV, v_half)
        v_rst = pl.op("scalar", lambda e: e.activation(out=r_rstd, in_=r_rstd,
                                                       func=AF.Exp), inc=eS)
        pl.wait("vector", eS, v_rst)
        v_msr = pl.op("vector", lambda e: e.tensor_mul(out=r_ms, in0=r_mean,
                                                       in1=r_rstd), inc=eV)
        pl.wait("sync", eV, v_msr)
        pl.dma("sync", ln_d[0:1, lo:lo + CBN], r_rstd, qs)
        v_rst_st = pl.dma("sync", ln_d[1:2, lo:lo + CBN], r_ms, qs)
        pl.wait("sync", qs, v_rst_st)
        pl.dma("sync", bc_rstd, ln_d[0:1, lo:lo + CBN].partition_broadcast(128), qs)
        vbc = pl.dma("sync", bc_ms, ln_d[1:2, lo:lo + CBN].partition_broadcast(128), qs)
        pl.wait("vector", qs, vbc)
        if cb == 0:
            pl.wait("vector", qy, v_w2_stores)
        for kt in range(C.NKD):
            pl.op("vector", lambda e, kt=kt: e.tensor_mul(
                out=hTc[:, kt * CBN:(kt + 1) * CBN],
                in0=hTc[:, kt * CBN:(kt + 1) * CBN], in1=bc_rstd))
            v_apply = pl.op("vector", lambda e, kt=kt, lo=lo: e.tensor_sub(
                out=xnT[:, kt * C.L + lo:kt * C.L + lo + CBN],
                in0=hTc[:, kt * CBN:(kt + 1) * CBN], in1=bc_ms),
                inc=eV if kt == C.NKD - 1 else None)
        pl.wait("scalar", eV, v_apply)
        pl.dma("scalar",
               xnT_d.rearrange("(k p) l -> p k l", p=128)[:, :, lo:lo + CBN],
               xnT.rearrange("p (k l) -> p k l", l=C.L)[:, :, lo:lo + CBN], qy)
    v_ln_done = v_apply
    v_xn_spill = pl.emitted.get(qy, 0)
    arF.release()

    pl.set_scope("m1")
    arF.mark()
    arB.mark()
    wsl_f = [arF.alloc(128, C.NKD * 128) for _ in range(4)]
    wsl_b = [arB.alloc(128, C.NKD * 128) for _ in range(4)]
    x_bf = [arB.alloc(128, C.L) for _ in range(2)]
    u_bf1 = [arB.alloc(128, C.L) for _ in range(2)]
    sz_bf1 = [arB.alloc(128, C.L) for _ in range(2)]
    diag4b = [arB.alloc(128, KC * 128) for _ in range(2)]
    LH = min(1024, C.L)
    NH = C.L // LH
    SL2 = min(512, LH)
    NSL2 = LH // SL2
    winT_r = winT.rearrange("(k p) e -> p k e", p=128)
    m1_marks = {}

    for j in range(C.NJ):
        par = j % 2
        px, pz = (pq[0], pq[1]) if par == 0 else (pq[2], pq[3])
        mk = m1_marks.get(j - 2, {})
        pl.wait("sync", eS, mk.get("wcvt"))
        if j == 0:
            pl.wait("sync", eV, v_ln_done)
        pl.dma("sync", wsl_f[2 * par], winT_r[:, :, j * 128:(j + 1) * 128], qs)
        v2 = pl.dma("sync", wsl_f[2 * par + 1],
                    winT_r[:, :, C.DIN + j * 128:C.DIN + (j + 1) * 128], qs)
        pl.wait("scalar", qs, v2)
        pl.wait("scalar", eT, mk.get("conv"))
        pl.op("scalar", lambda e, p=par: e.copy(out=wsl_b[2 * p], in_=wsl_f[2 * p]))
        v_wcvt = pl.op("scalar", lambda e, p=par: e.copy(
            out=wsl_b[2 * p + 1], in_=wsl_f[2 * p + 1]), inc=eS)
        pl.wait("scalar", eT, mk.get("conv"))
        pl.wait("scalar", qy, mk.get("spill", 0))
        pl.wait("tensor", eS, v_wcvt)
        pl.wait("tensor", eV, v_ln_done)
        v_xev = None
        for h2 in range(NH):
            vx = None
            for kt in range(C.NKD):
                for si in range(NSL2):
                    vx = pl.op("tensor",
                        lambda e, p=par, kt=kt, h2=h2, s_=si * SL2, px=px: e.matmul(
                            out=px[:, s_:s_ + SL2],
                            lhsT=wsl_b[2 * p][:, kt * 128:(kt + 1) * 128],
                            rhs=xnT[:, kt * C.L + h2 * LH + s_:
                                    kt * C.L + h2 * LH + s_ + SL2],
                            start=(kt == 0), stop=(kt == C.NKD - 1)),
                        inc=eT if (kt == C.NKD - 1 and si == NSL2 - 1) else None)
            pl.wait("scalar", eT, vx)
            v_xev = pl.op("scalar", lambda e, p=par, h2=h2, px=px: e.copy(
                out=x_bf[p][:, h2 * LH:(h2 + 1) * LH], in_=px[:, 0:LH]), inc=eS)
            if h2 + 1 < NH:
                pl.wait("tensor", eS, v_xev)
        v_szev = None
        for h2 in range(NH):
            vz = None
            for kt in range(C.NKD):
                for si in range(NSL2):
                    vz = pl.op("tensor",
                        lambda e, p=par, kt=kt, h2=h2, s_=si * SL2, pz=pz: e.matmul(
                            out=pz[:, s_:s_ + SL2],
                            lhsT=wsl_b[2 * p + 1][:, kt * 128:(kt + 1) * 128],
                            rhs=xnT[:, kt * C.L + h2 * LH + s_:
                                    kt * C.L + h2 * LH + s_ + SL2],
                            start=(kt == 0), stop=(kt == C.NKD - 1)),
                        inc=eT if (kt == C.NKD - 1 and si == NSL2 - 1) else None)
            pl.wait("scalar", eT, vz)
            v_szev = pl.op("scalar", lambda e, p=par, h2=h2, pz=pz: e.activation(
                out=sz_bf1[p][:, h2 * LH:(h2 + 1) * LH], in_=pz[:, 0:LH],
                func=AF.Silu), inc=eS)
            if h2 + 1 < NH:
                pl.wait("tensor", eS, v_szev)
        pl.wait("vector", qs, v_consts)
        if j >= 2:
            pl.wait("vector", eT, m1_marks[j - 2]["conv"])
        diag4 = diag4b[par]
        vd = None
        for k in range(KC):
            vd = pl.op("vector", lambda e, k=k, j=j, diag4=diag4: e.tensor_scalar(
                out=diag4[:, k * 128:(k + 1) * 128], in0=i128[:],
                scalar1=convw_s[:, j * KC + k:j * KC + k + 1], scalar2=None,
                op0=OP.mult), inc=eV if k == KC - 1 else None)
        pl.wait("tensor", eS, v_xev)
        pl.wait("tensor", eV, vd)
        v_uev = None
        vc = None
        CVS = min(512, LH)
        for h2 in range(NH):
            for q in range(LH // CVS):
                for k in range(KC):
                    sft = (KC - 1) - k
                    a = h2 * LH + q * CVS
                    o_lo = max(a, sft)
                    n_ = a + CVS - o_lo
                    vc = pl.op("tensor",
                        lambda e, p=par, k=k, o_lo=o_lo, n_=n_, sft=sft, px=px, h2=h2, diag4=diag4: e.matmul(
                            out=px[:, o_lo - h2 * LH:o_lo - h2 * LH + n_],
                            lhsT=diag4[:, k * 128:(k + 1) * 128],
                            rhs=x_bf[p][:, o_lo - sft:o_lo - sft + n_],
                            start=(k == 0), stop=(k == KC - 1)),
                        inc=eT if (k == KC - 1 and q == LH // CVS - 1) else None)
            pl.wait("scalar", eT, vc)
            v_uev = pl.op("scalar", lambda e, p=par, j=j, h2=h2, px=px: e.activation(
                out=u_bf1[p][:, h2 * LH:(h2 + 1) * LH], in_=px[:, 0:LH],
                func=AF.Silu, bias=convb_s[:, j:j + 1]), inc=eS)
            if h2 + 1 < NH:
                pl.wait("tensor", eS, v_uev)
        pl.wait("scalar", eS, v_uev)
        pl.dma("scalar", u_d[j * 128:(j + 1) * 128, :], u_bf1[par], qy)
        pl.wait("scalar", eS, v_szev)
        v_sp = pl.dma("scalar", sz_d[j * 128:(j + 1) * 128, :], sz_bf1[par], qy)
        m1_marks[j] = {"wcvt": v_wcvt, "conv": vc, "uev": v_uev, "spill": v_sp}
    v_m1_S = m1_marks[C.NJ - 1]["uev"]
    v_m1_sp = m1_marks[C.NJ - 1]["spill"]

    pl.set_scope("m1b")
    arB.release()
    arF.release()
    arB.mark()
    arF.mark()
    wx_f = arF.alloc(128, C.NJ * EDBL)
    wx_bf = arB.alloc(128, C.NJ * EDBL)
    ubk = [arB.alloc(128, C.L) for _ in range(2)]
    bcev = arB.alloc(2 * NST, C.L)
    pl.wait("sync", eS, m1_marks[C.NJ - 1]["wcvt"])
    vq = pl.dma("sync", wx_f.rearrange("p (j e) -> p j e", e=EDBL),
                wxT.rearrange("(j p) e -> p j e", p=128)[:], qs)
    pl.wait("scalar", qs, vq)
    pl.wait("scalar", eT, m1_marks[C.NJ - 1]["conv"])
    v_wxc = pl.op("scalar", lambda e: e.copy(out=wx_bf, in_=wx_f), inc=eS)
    NQ = C.L // min(512, C.L)
    SQ = C.L // NQ
    vmmJ = {}
    pl.wait("sync", qy, v_m1_sp)
    for j in range(C.NJ):
        if j >= 2:
            pl.wait("sync", eT, vmmJ[j - 2])
        v_uld = pl.dma("sync", ubk[j % 2], u_d[j * 128:(j + 1) * 128, :], qa)
        pl.wait("tensor", qa, v_uld)
        if j == 0:
            pl.wait("tensor", eS, v_wxc)
            pl.wait("tensor", eS, v_m1_S)
            pl.wait("tensor", eS, v_wdtc)
        vlast = None
        for qi in range(NQ):
            s_ = qi * SQ
            pdt = pq[0] if s_ < 1024 else pq[1]
            pl.op("tensor", lambda e, j=j, s_=s_, pdt=pdt: e.matmul(
                out=pdt[0:RDT, s_ % 1024:s_ % 1024 + SQ],
                lhsT=wx_bf[:, j * EDBL:j * EDBL + RDT],
                rhs=ubk[j % 2][:, s_:s_ + SQ],
                start=(j == 0), stop=(j == C.NJ - 1)))
        for qi in range(NQ):
            s_ = qi * SQ
            pbc = pq[2] if s_ < 1024 else pq[3]
            vlast = pl.op("tensor", lambda e, j=j, s_=s_, pbc=pbc: e.matmul(
                out=pbc[0:2 * NST, s_ % 1024:s_ % 1024 + SQ],
                lhsT=wx_bf[:, j * EDBL + RDT:(j + 1) * EDBL],
                rhs=ubk[j % 2][:, s_:s_ + SQ],
                start=(j == 0), stop=(j == C.NJ - 1)), inc=eT)
        vmmJ[j] = vlast
    v_xdbl = vmmJ[C.NJ - 1]
    pl.wait("scalar", eT, v_xdbl)
    for qi in range(NQ):
        s_ = qi * SQ
        pdt = pq[0] if s_ < 1024 else pq[1]
        pl.op("scalar", lambda e, s_=s_, pdt=pdt: e.copy(
            out=dtr_bf[:, s_:s_ + SQ], in_=pdt[0:RDT, s_ % 1024:s_ % 1024 + SQ]))
    v_bcev = None
    for qi in range(NQ):
        s_ = qi * SQ
        pbc = pq[2] if s_ < 1024 else pq[3]
        v_bcev = pl.op("scalar", lambda e, s_=s_, pbc=pbc: e.copy(
            out=bcev[:, s_:s_ + SQ], in_=pbc[0:2 * NST, s_ % 1024:s_ % 1024 + SQ]),
            inc=eS if qi == NQ - 1 else None)
    pl.wait("scalar", eS, v_bcev)
    pl.dma("scalar", bc_d[:, 0:C.L], bcev[0:NST, :], qy)
    v_bcst = pl.dma("scalar", bc_d[:, C.L:2 * C.L], bcev[NST:2 * NST, :], qy)
    arB.release()
    arF.release()
    arB.release()

    arF.mark()
    arB.mark()
    delta_f = arF.alloc(128, C.CL)
    ch1 = arF.alloc(128, min(512, C.CL))
    ch2 = arF.alloc(128, min(512, C.CL))
    dA2 = [arF.alloc(128, C.CL) for _ in range(2)]
    out_sb = [arF.alloc(128, C.CBM) for _ in range(2)]
    hT_t = [arF.alloc(128, C.CBM) for _ in range(2)]
    du_b = arB.alloc(128, C.CL)
    dbu2 = [arB.alloc(128, C.CL) for _ in range(2)]
    h2 = [arB.alloc(128, C.CL) for _ in range(2)]
    ytmp = [arB.alloc(128, C.CL) for _ in range(4)]
    bcr = [arB.alloc(128, 2 * C.CL) for _ in range(3)]
    u_sc = [arB.alloc(128, C.CL) for _ in range(2)]
    sz_sc = [arB.alloc(128, C.CL) for _ in range(2)]
    ybf = [arB.alloc(128, C.CL) for _ in range(2)]
    xn_fc = arB.alloc(128, C.NKD * C.CB)
    h_ffn = arB.alloc(128, C.NOT * C.CB)
    sg_b = arB.alloc(128, C.CB)
    th_f = arF.alloc(128, C.CB)
    th_a = arF.alloc(128, C.CB)
    gwsl = [arB.alloc(128, C.NKD * 128) for _ in range(2)]
    uwsl = [arB.alloc(128, C.NKD * 128) for _ in range(2)]
    NOTH = (C.NOT + 3) // 4
    dwsl = arB.alloc(128, NOTH * 128)
    w2sl_b = arB.alloc(128, C.NJ * 128)
    ysl = arB.alloc(128, C.NJ * C.CBM)
    gt_t = [arB.alloc(128, C.CBM) for _ in range(2)]

    pY = pq[0]
    pDl = pq[1][:, 0:512]
    pFg2 = [pq[2][:, 0:512], pq[2][:, 512:1024]]
    pFu2 = [pq[1][:, 512:1024], pq[3][:, 512:1024]]
    pM = pq[3][:, 0:512]

    CBL = C.CL // C.CB

    fstate = {
        "gt_store": {}, "yspill": {}, "yslot": {},
        "vhm_last": 0, "vsg_last": 0, "vdownev": 0, "vdown_last": 0,
        "vcomb": 0, "vout_st": {}, "vgt_read": 0,
        "vxn_free": 0, "vslab": {}, "vdsl": 0, "vw2sl": 0, "vysl": 0,
    }

    def gen_ffn(fc):
        pl.set_scope("ffn")
        lo = fc * C.CB
        pl.wait("sync", qy, v_xn_spill)
        pl.wait("sync", eT, fstate["vxn_free"])
        vxn = pl.dma(
            "sync", xn_fc.rearrange("p (k l) -> p k l", l=C.CB),
            xnT_d.rearrange("(k p) l -> p k l", p=128)[:, :, lo:lo + C.CB],
            qa)
        yield
        vg = {}
        vu = {}
        vth = {}

        def emit_mms(g):
            gp = g % 2
            c_, ot_l = divmod(g, C.OTP)
            base = (c_ * C.OTP + ot_l) * C.NKD
            pl.wait("sync", cc, 3)
            if g >= 2:
                pl.wait("sync", eT, vu[g - 2])
            pl.dma("sync", gwsl[gp].rearrange("p (t e) -> p t e", e=128),
                   wg_d.rearrange("t p e -> p t e")[:, base:base + C.NKD, :],
                   qa)
            return pl.dma("sync", uwsl[gp].rearrange("p (t e) -> p t e", e=128),
                          wu_d.rearrange("t p e -> p t e")[:, base:base + C.NKD, :],
                          qa)

        def emit_gu(g, vsl):
            gp = g % 2
            pl.wait("tensor", qa, vsl)
            pl.wait("tensor", qa, vxn)
            pl.wait("tensor", eV, fstate["vsg_last"] if g < 2 else vhm_g.get(g - 2, 0))
            if g == 0 and fc == 0:
                pl.wait("tensor", eS, v_bcev)
            for kt in range(C.NKD):
                vg[g] = pl.op("tensor", lambda e, kt=kt, gp=gp: e.matmul(
                    out=pFg2[gp][:, 0:C.CB], lhsT=gwsl[gp][:, kt * 128:(kt + 1) * 128],
                    rhs=xn_fc[:, kt * C.CB:(kt + 1) * C.CB],
                    start=(kt == 0), stop=(kt == C.NKD - 1)),
                    inc=eT if kt == C.NKD - 1 else None)
            for kt in range(C.NKD):
                vu[g] = pl.op("tensor", lambda e, kt=kt, gp=gp: e.matmul(
                    out=pFu2[gp][:, 0:C.CB], lhsT=uwsl[gp][:, kt * 128:(kt + 1) * 128],
                    rhs=xn_fc[:, kt * C.CB:(kt + 1) * C.CB],
                    start=(kt == 0), stop=(kt == C.NKD - 1)),
                    inc=eT if kt == C.NKD - 1 else None)

        def emit_act(g):
            gp = g % 2
            pl.wait("scalar", eT, vg[g])
            pl.wait("scalar", eV, fstate["vsg_last"])
            vth[g] = pl.op("scalar", lambda e, gp=gp: e.activation(
                out=th_f, in_=pFg2[gp][:, 0:C.CB], func=AF.Tanh, scale=0.5), inc=eS)

        vhm_g = {}

        def emit_hm(g):
            gp = g % 2
            pl.wait("vector", eS, vth[g])
            pl.wait("vector", eV, fstate["vhm_last"])
            if g == 0:
                pl.wait("vector", qy, fstate["gt_store"].get((fc - 1, C.NMT - 1), 0))
            pl.op("vector", lambda e: e.tensor_scalar(
                out=th_a, in0=th_f, scalar1=1.0, scalar2=0.5,
                op0=OP.add, op1=OP.mult))
            vsg = pl.op("vector", lambda e, gp=gp: e.tensor_mul(
                out=sg_b, in0=th_a, in1=pFg2[gp][:, 0:C.CB]), inc=eV)
            fstate["vsg_last"] = vsg
            pl.wait("vector", eT, vu[g])
            if g == 0:
                pl.wait("vector", eT, fstate["vdown_last"])
            vhm = pl.op("vector", lambda e, g=g, gp=gp: e.tensor_mul(
                out=h_ffn[:, g * C.CB:(g + 1) * C.CB], in0=sg_b,
                in1=pFu2[gp][:, 0:C.CB]), inc=eV)
            vhm_g[g] = vhm
            fstate["vhm_last"] = vhm

        vsl_next = emit_mms(0)
        emit_gu(0, vsl_next)
        yield
        for g in range(1, C.NOT):
            vsl_next = emit_mms(g)
            yield
            emit_gu(g, vsl_next)
            yield
            emit_act(g - 1)
            emit_hm(g - 1)
            yield
        emit_act(C.NOT - 1)
        emit_hm(C.NOT - 1)
        yield
        fstate["vxn_free"] = vu[C.NOT - 1]
        vhm_all = fstate["vhm_last"]
        for mt in range(C.NMT):
            c_, mt_l = divmod(mt, nmt_l)
            base = (c_ * nmt_l + mt_l) * C.NOT
            vdn = None
            for half in range(4):
                k0 = half * NOTH
                k1 = min(C.NOT, k0 + NOTH)
                if k1 <= k0:
                    continue
                pl.wait("sync", eT, fstate["vdsl"])
                vdsl = pl.dma("sync", dwsl.rearrange("p (t e) -> p t e", e=128)
                              [:, 0:k1 - k0, :],
                              wd_d.rearrange("t p e -> p t e")
                              [:, base + k0:base + k1, :], qa)
                yield
                pl.wait("tensor", qa, vdsl)
                pl.wait("tensor", eV, vhm_all)
                if half == 0:
                    pl.wait("tensor", eV, fstate["vdownev"])
                    pl.wait("tensor", eV, fstate["vsg_last"])
                for ko in range(k0, k1):
                    vdn = pl.op("tensor", lambda e, ko=ko, k0=k0: e.matmul(
                        out=pFg2[0][:, 0:C.CB],
                        lhsT=dwsl[:, (ko - k0) * 128:(ko - k0 + 1) * 128],
                        rhs=h_ffn[:, ko * C.CB:(ko + 1) * C.CB],
                        start=(ko == 0), stop=(ko == C.NOT - 1)),
                        inc=eT if ko == k1 - 1 else None)
                    if ko % 8 == 7:
                        yield
                fstate["vdsl"] = vdn
            if mt == C.NMT - 1:
                fstate["vdown_last"] = vdn
            pl.wait("scalar", eT, vdn)
            pl.wait("scalar", eV, fstate["vsg_last"])
            vth2 = pl.op("scalar", lambda e: e.activation(
                out=th_f, in_=pFg2[0][:, 0:C.CB], func=AF.Tanh, scale=0.5), inc=eS)
            pl.wait("vector", eS, vth2)
            pl.wait("vector", eV, fstate["vhm_last"])
            pl.wait("vector", qy, fstate["gt_store"].get((fc, mt - 1), 0))
            pl.op("vector", lambda e: e.tensor_scalar(
                out=th_a, in0=th_f, scalar1=1.0, scalar2=0.5,
                op0=OP.add, op1=OP.mult))
            vgt = pl.op("vector", lambda e: e.tensor_mul(
                out=sg_b, in0=th_a, in1=pFg2[0][:, 0:C.CB]), inc=eV)
            fstate["vdownev"] = vgt
            fstate["vsg_last"] = vgt
            pl.wait("scalar", eV, vgt)
            fstate["gt_store"][(fc, mt)] = pl.dma(
                "scalar", gT_d[mt * 128:(mt + 1) * 128, lo:lo + C.CB], sg_b, qy)
            yield

    def gen_m4(s):
        pl.set_scope("m4")
        lc = (s * C.CBM) // C.CL
        lo = s * C.CBM
        CBM = C.CBM
        pl.wait("sync", qy, fstate["yspill"][lc])
        pl.wait("sync", eT, fstate["vysl"])
        vy = pl.dma("sync", ysl.rearrange("p (k l) -> p k l", l=CBM),
                    y_d.rearrange("(k p) l -> p k l", p=128)[:, :, lo:lo + CBM],
                    qa)
        yield
        fcg = lo // C.CB
        for mt in range(C.NMT):
            mp = mt % 2
            c_ = (mt * 128) // C.DMS
            colo = (mt * 128) % C.DMS
            pl.wait("sync", cc, v_cc4)
            pl.wait("sync", eT, fstate["vw2sl"])
            vw2 = pl.dma("sync", w2sl_b.rearrange("p (k e) -> p k e", e=128),
                         w2_d.rearrange("(r k) p e -> r p k e", k=C.NJ)
                         [c_, :, :, colo:colo + 128], qa)
            pl.wait("sync", qy, fstate["gt_store"][(fcg, mt)])
            pl.wait("sync", eV, fstate["vgt_read"])
            pl.wait("sync", qy, fstate["vout_st"].get(mp, 0))
            pl.dma("sync", gt_t[mp], gT_d[mt * 128:(mt + 1) * 128, lo:lo + CBM], qs)
            vht = pl.dma("sync", hT_t[mp],
                         hT[mt * 128:(mt + 1) * 128, lo:lo + CBM], qs)
            yield
            pl.wait("tensor", qa, vy)
            pl.wait("tensor", qa, vw2)
            pl.wait("tensor", eV, fstate["vcomb"])
            vmm = None
            for kt in range(C.NJ):
                vmm = pl.op("tensor", lambda e, kt=kt: e.matmul(
                    out=pM[:, 0:CBM], lhsT=w2sl_b[:, kt * 128:(kt + 1) * 128],
                    rhs=ysl[:, kt * CBM:(kt + 1) * CBM],
                    start=(kt == 0), stop=(kt == C.NJ - 1)),
                    inc=eT if kt == C.NJ - 1 else None)
                if kt % 8 == 7:
                    yield
            fstate["vw2sl"] = vmm
            fstate["vysl"] = vmm
            pl.wait("vector", eT, vmm)
            pl.wait("vector", qs, vht)
            pl.wait("vector", qy, fstate["vout_st"].get(mp, 0))
            pl.op("vector", lambda e, mp=mp: e.tensor_mul(
                out=gt_t[mp], in0=gt_t[mp], in1=pM[:, 0:CBM]))
            vcb = pl.op("vector", lambda e, mp=mp: e.tensor_add(
                out=out_sb[mp], in0=gt_t[mp], in1=hT_t[mp]), inc=eV)
            fstate["vcomb"] = vcb
            fstate["vgt_read"] = vcb
            pl.wait("scalar", eV, vcb)
            fstate["vout_st"][mp] = pl.dma(
                "scalar", outT[mt * 128:(mt + 1) * 128, lo:lo + CBM],
                out_sb[mp], qy)
            yield

    genq = deque()
    for fc in range(C.NCB):
        genq.append(gen_ffn(fc))

    def pump(k=1):
        for _ in range(k):
            while genq:
                try:
                    next(genq[0])
                    return
                except StopIteration:
                    genq.popleft()
            return

    vstep = {}
    v_scan = {}
    v_dbu = {}
    v_ym = {}
    v_car = {}
    vyadd = {}
    vtD = {}
    vgate = {}
    vspU = {}
    v_bc = {}
    v_du = {}
    v_dAw = {}
    NTOT = C.NLC * C.NJ

    def emit_bcast(st_):
        if st_ >= NTOT * NST:
            return
        lc2, r = divmod(st_, C.NJ * NST)
        j2, n2 = divmod(r, NST)
        par = st_ % 3
        pl.wait("sync", qy, v_bcst)
        pl.wait("sync", cg, v_ym.get(st_ - 3, 0))
        lo2 = lc2 * C.CL
        src = bc_d[n2:n2 + 1, :].rearrange("p (s l) -> p s l", s=2)[:, :, lo2:lo2 + C.CL]
        v_bc[st_] = pl.dma("sync",
                           bcr[par].rearrange("p (s l) -> p s l", s=2),
                           src.partition_broadcast(128), qb)

    def emit_uld(t2):
        if t2 >= NTOT:
            return
        lc2, j2 = divmod(t2, C.NJ)
        par = t2 % 2
        pl.wait("sync", qy, v_m1_sp)
        pl.wait("sync", eV, v_du.get(t2 - 2, 0))
        pl.wait("sync", eT, vtD.get(t2 - 2, 0))
        pl.wait("sync", eV, vgate.get(t2 - 2, 0))
        lo2 = lc2 * C.CL
        pl.dma("sync", u_sc[par], u_d[j2 * 128:(j2 + 1) * 128, lo2:lo2 + C.CL], qa)
        vspU[t2] = pl.dma("sync", sz_sc[par],
                          sz_d[j2 * 128:(j2 + 1) * 128, lo2:lo2 + C.CL], qa)

    for _st in range(3):
        emit_bcast(_st)
    emit_uld(0)
    emit_uld(1)

    for lc in range(C.NLC):
        lo = lc * C.CL
        for j in range(C.NJ):
            t = lc * C.NJ + j
            pl.set_scope(f"scan{lc}")
            par = t % 2
            NQ2 = C.CL // min(512, C.CL)
            SQ2 = C.CL // NQ2
            vsp = None
            for si in range(NQ2):
                s_ = si * SQ2
                pl.wait("tensor", eS, v_dAw.get((t, si - 1), v_dAw.get((t - 1, NQ2 - 1), 0)))
                if t == 0 and si == 0:
                    pl.wait("tensor", eS, v_bcev)
                vdt = pl.op("tensor", lambda e, j=j, lo=lo, s_=s_: e.matmul(
                    out=pDl[:, 0:SQ2],
                    lhsT=wdt_bf[:, j * 128:(j + 1) * 128],
                    rhs=dtr_bf[:, lo + s_:lo + s_ + SQ2],
                    start=True, stop=True), inc=eT)
                pl.wait("scalar", eT, vdt)
                if si == 0:
                    pl.wait("scalar", eV, v_du.get(t - 1, 0))
                pl.op("scalar", lambda e, j=j, s_=s_: e.activation(
                    out=delta_f[:, s_:s_ + SQ2], in_=pDl[:, 0:SQ2],
                    func=AF.Identity, bias=dtb_s[:, j:j + 1]))
                if si > 0:
                    pl.wait("scalar", eV, vchadd)
                v_dAw[(t, si)] = pl.op("scalar", lambda e, s_=s_: e.activation(
                    out=ch1[:, 0:SQ2], in_=delta_f[:, s_:s_ + SQ2],
                    func=AF.Abs), inc=eS)
                pl.op("scalar", lambda e: e.activation(
                    out=ch2[:, 0:SQ2], in_=ch1[:, 0:SQ2],
                    func=AF.Exp, scale=-1.0))
                pl.op("scalar", lambda e: e.activation(
                    out=ch1[:, 0:SQ2], in_=ch2[:, 0:SQ2],
                    func=AF.Ln, bias=1.0))
                vsp = pl.op("scalar", lambda e, s_=s_: e.activation(
                    out=ch2[:, 0:SQ2], in_=delta_f[:, s_:s_ + SQ2],
                    func=AF.Relu), inc=eS)
                pl.wait("vector", eS, vsp)
                vchadd = pl.op("vector", lambda e, s_=s_: e.tensor_add(
                    out=delta_f[:, s_:s_ + SQ2], in0=ch1[:, 0:SQ2],
                    in1=ch2[:, 0:SQ2]), inc=eV)
            pl.wait("vector", qa, vspU[t])
            pl.wait("vector", cg, v_dbu.get(t * NST - 1, 0))
            v_du[t] = pl.op("vector", lambda e, par=par: e.tensor_mul(
                out=du_b, in0=delta_f, in1=u_sc[par]), inc=eV)
            pl.wait("vector", eT, vtD.get(t - 1, 0))
            vdg = pl.op("vector", lambda e, j=j: e.tensor_scalar(
                out=diagD[:], in0=i128[:],
                scalar1=dvec_s[:, j:j + 1], scalar2=None, op0=OP.mult), inc=eV)
            for n in range(NST):
                st_ = t * NST + n
                sp = st_ % 2
                spy = st_ % 4
                sp4 = st_ % 3
                col = j * NST + n
                pl.wait("scalar", eV, v_du[t])
                pl.wait("scalar", eV, v_scan.get(st_ - 2, 0))
                vda = pl.op("scalar", lambda e, j=j, n=n, sp=sp: e.activation(
                    out=dA2[sp], in_=delta_f, func=AF.Exp,
                    scale=negA[:, (j * NST + n):(j * NST + n) + 1]), inc=eS)
                pl.wait("gpsimd", eV, v_du[t])
                pl.wait("gpsimd", qb, v_bc[st_])
                pl.wait("gpsimd", eV, v_scan.get(st_ - 2, 0))
                v_dbu[st_] = pl.op("gpsimd", lambda e, sp=sp, sp4=sp4: e.tensor_tensor(
                    out=dbu2[sp], in0=du_b, in1=bcr[sp4][:, 0:C.CL],
                    op=OP.mult), inc=cg)
                pl.wait("vector", eS, vda)
                pl.wait("vector", cg, v_dbu[st_])
                pl.wait("vector", cg, v_ym.get(st_ - 2, 0))
                if lc == 0:
                    v_scan[st_] = pl.op("vector", lambda e, sp=sp: e.tensor_tensor_scan(
                        out=h2[sp], data0=dA2[sp], data1=dbu2[sp], initial=0.0,
                        op0=OP.mult, op1=OP.add), inc=eV)
                else:
                    pl.wait("vector", cg, v_car.get(col, 0))
                    v_scan[st_] = pl.op("vector", lambda e, sp=sp, col=col: e.tensor_tensor_scan(
                        out=h2[sp], data0=dA2[sp], data1=dbu2[sp],
                        initial=carry[:, col:col + 1], op0=OP.mult, op1=OP.add),
                        inc=eV)
                pl.wait("gpsimd", eV, v_scan[st_])
                pl.wait("gpsimd", eT, vyadd.get(st_ - 4, 0))
                vy1 = pl.op("gpsimd", lambda e, sp=sp, spy=spy, sp4=sp4: e.tensor_tensor(
                    out=ytmp[spy], in0=h2[sp], in1=bcr[sp4][:, C.CL:2 * C.CL],
                    op=OP.mult), inc=cg)
                if lc + 1 < C.NLC:
                    vy1 = pl.op("gpsimd", lambda e, sp=sp, col=col: e.tensor_copy(
                        out=carry[:, col:col + 1], in_=h2[sp][:, C.CL - 1:C.CL]),
                        inc=cg)
                    v_car[col] = vy1
                v_ym[st_] = vy1
                vstep[st_] = v_scan[st_]
                if DBG_TAPS and t == DBG_T and n == 0:
                    pl.wait("sync", cg, v_ym[st_])
                    pl.dma("sync", scd_d[0], delta_f, qs)
                    pl.dma("sync", scd_d[1], dA2[sp], qs)
                    pl.dma("sync", scdB_d[0], dbu2[sp], qs)
                    pl.dma("sync", scdB_d[1], h2[sp], qs)
                    pl.dma("sync", car_d[:], carry[:], qs)
                    vdbg2 = pl.dma("sync", scdB_d[2], ytmp[spy], qs)
                    pl.wait("vector", qs, vdbg2)
                pl.wait("tensor", cg, v_ym[st_])
                if n == 0:
                    pl.wait("tensor", eV, vgate.get(t - 1, 0))
                for ci in range(CBL):
                    vyadd[st_] = pl.op("tensor", lambda e, spy=spy, ci=ci, n=n: e.matmul(
                        out=pY[:, ci * C.CB:(ci + 1) * C.CB], lhsT=i128[:],
                        rhs=ytmp[spy][:, ci * C.CB:(ci + 1) * C.CB],
                        start=(n == 0), stop=False),
                        inc=eT if ci == CBL - 1 else None)
                emit_bcast(st_ + 3)
                sc_ = pl.cur_scope
                pump(1)
                pump(1)
                pl.set_scope(sc_)
            pl.wait("tensor", eV, vdg)
            vD = None
            for ci in range(CBL):
                vD = pl.op("tensor", lambda e, par=par, ci=ci: e.matmul(
                    out=pY[:, ci * C.CB:(ci + 1) * C.CB], lhsT=diagD[:],
                    rhs=u_sc[par][:, ci * C.CB:(ci + 1) * C.CB],
                    start=False, stop=True),
                    inc=eT if ci == CBL - 1 else None)
            vtD[t] = vD
            pl.wait("vector", eT, vD)
            pl.wait("vector", qa, vspU[t])
            rp = t % 2
            pl.wait("vector", qy, fstate["yslot"].get(t - 2, 0))
            vgate[t] = pl.op("vector", lambda e, par=par, rp=rp: e.tensor_mul(
                out=ybf[rp], in0=pY[:, 0:C.CL], in1=sz_sc[par]), inc=eV)
            pl.wait("scalar", eV, vgate[t])
            vys = pl.dma("scalar", y_d[j * 128:(j + 1) * 128, lo:lo + C.CL],
                         ybf[rp], qy)
            fstate["yslot"][t] = vys
            fstate["yspill"][lc] = vys
            emit_uld(t + 2)
            sc_ = pl.cur_scope
            pump(1)
            pl.set_scope(sc_)
        for ci in range(C.CL // C.CBM):
            genq.append(gen_m4(lc * (C.CL // C.CBM) + ci))

    while genq:
        try:
            next(genq[0])
        except StopIteration:
            genq.popleft()

    if debug:
        dbgs = [("w2sl_src", w2_sl.ap(), [C.NJ, 128, C.DMS], BF16),
                ("car_d", car_d, [128, C.NJ * C.NST], F32),
                ("scd_d", scd_d, [2, 128, C.CL], F32),
                ("scdB_d", scdB_d, [3, 128, C.CL], BF16),
                ("xnT_d", xnT_d, [C.DM, C.L], BF16),
                ("u_d", u_d, [C.DIN, C.L], BF16),
                ("sz_d", sz_d, [C.DIN, C.L], BF16),
                ("bc_d", bc_d, [2 * NST, C.L], BF16),
                ("y_d", y_d, [C.DIN, C.L], BF16),
                ("gT_d", gT_d, [C.DM, C.L], BF16),
                ("w2_d", w2_d, [C.NC * C.NJ, 128, C.DMS], BF16),
                ("wg_d", wg_d, [C.NC * C.OTP * C.NKD, 128, 128], BF16)]
        pl.wait("sync", qa, pl.emitted.get(qa, 0))
        pl.wait("sync", qs, pl.emitted.get(qs, 0))
        pl.wait("sync", qb, pl.emitted.get(qb, 0))
        pl.wait("sync", qc2, pl.emitted.get(qc2, 0))
        for nm, src, shp, dt in dbgs:
            dst = nc.dram_tensor(nm + "_dbg", shp, dt, kind="ExternalOutput").ap()
            pl.dma("sync", dst[:], src[:], qs)
    for q in (qa, qy, qs, qb, qc2, cc):
        pl.wait("sync", q, pl.emitted.get(q, 0))

    for name in sorted(pl.emitted.keys()):
        SEMS[name] = es.enter_context(nc.semaphore(name))
    def replay(e, eng):
        cur = None
        ctx = None
        for f, sc in zip(pl.ops[eng], pl.scopes[eng]):
            if sc != cur:
                if ctx is not None:
                    ctx.__exit__(None, None, None)
                ctx = nc.named_scope(f"{sc}_{eng[0]}")
                ctx.__enter__()
                cur = sc
            f(e)
        if ctx is not None:
            ctx.__exit__(None, None, None)

    with nc.Block() as block:
        @block.sync
        def _(e):
            replay(e, "sync")

        @block.vector
        def _(e):
            replay(e, "vector")

        @block.scalar
        def _(e):
            replay(e, "scalar")

        @block.tensor
        def _(e):
            replay(e, "tensor")

        @block.gpsimd
        def _(e):
            replay(e, "gpsimd")
    es.close()
    return nc


def _hadamard(n):
    H = np.array([[1.0]], dtype=np.float32)
    while H.shape[0] < n:
        H = np.block([[H, H], [H, -H]])
    return H


def host_inputs(cfg: Cfg, core, hidden, W_in, conv_w, conv_b, W_x, W_dt,
                dt_bias, A_log, D, W_out, lut, gate_idx, up_idx, down_idx):
    C = cfg
    bf = ml_dtypes.bfloat16
    f32 = np.float32
    H = (_hadamard(C.DIN) / math.sqrt(C.DIN)).astype(f32)
    Ht = np.ascontiguousarray(
        H.reshape(C.NJ, 128, C.NJ, 128).transpose(2, 0, 1, 3)).astype(bf)
    gi = np.zeros((C.DM, C.OTP * C.NC * 128), np.int32)
    gi[:, 0:C.INTER] = gate_idx.T
    ui = np.zeros((C.DM, C.OTP * C.NC * 128), np.int32)
    ui[:, 0:C.INTER] = up_idx.T
    di = np.ascontiguousarray(down_idx.T.astype(np.int32))
    o0 = core * C.OTP * 128
    m0 = core * C.DMS
    return {
        "hT": np.ascontiguousarray(hidden[core].T).astype(f32),
        "winT": np.ascontiguousarray(W_in.T).astype(f32),
        "wxT": np.ascontiguousarray(W_x.T).astype(f32),
        "wdtT": np.ascontiguousarray(W_dt.T).astype(f32),
        "wosl": np.ascontiguousarray(W_out.T[:, m0:m0 + C.DMS]).astype(f32),
        "convw": np.ascontiguousarray(
            conv_w.reshape(C.NJ, 128, C.KC).transpose(1, 0, 2)
            .reshape(128, C.NJ * C.KC)).astype(f32),
        "convb": np.ascontiguousarray(
            conv_b.reshape(C.NJ, 128).T).astype(f32),
        "dtb": np.ascontiguousarray(
            dt_bias.reshape(C.NJ, 128).T).astype(f32),
        "alog": np.ascontiguousarray(
            A_log.reshape(C.NJ, 128, C.NST).transpose(1, 0, 2)
            .reshape(128, C.NJ * C.NST)).astype(f32),
        "dvec": np.ascontiguousarray(
            D.reshape(C.NJ, 128).T).astype(f32),
        "lutv": np.ascontiguousarray(lut.reshape(1, -1)).astype(f32),
        "gidx": np.ascontiguousarray(gi[:, o0:o0 + C.OTP * 128]),
        "uidx": np.ascontiguousarray(ui[:, o0:o0 + C.OTP * 128]),
        "didx": np.ascontiguousarray(di[:, m0:m0 + C.DMS]),
        "Ht": Ht,
        "I128": np.eye(128, dtype=bf),
        "ones128": np.ones((128, 1), np.float32),
    }


def kernel(L0_attn_hidden_states, L0_attn_W_in, L0_attn_conv_w, L0_attn_conv_b,
           L0_attn_W_x, L0_attn_W_dt, L0_attn_dt_bias, L0_attn_A_log,
           L0_attn_D, L0_attn_W_out, L0_ffn_lut, L0_ffn_gate_idx,
           L0_ffn_up_idx, L0_ffn_down_idx):
    C = FULL
    args = (np.asarray(L0_attn_hidden_states), np.asarray(L0_attn_W_in),
            np.asarray(L0_attn_conv_w), np.asarray(L0_attn_conv_b),
            np.asarray(L0_attn_W_x), np.asarray(L0_attn_W_dt),
            np.asarray(L0_attn_dt_bias), np.asarray(L0_attn_A_log),
            np.asarray(L0_attn_D), np.asarray(L0_attn_W_out),
            np.asarray(L0_ffn_lut), np.asarray(L0_ffn_gate_idx),
            np.asarray(L0_ffn_up_idx), np.asarray(L0_ffn_down_idx))
    nc = build(C)
    in_maps = [host_inputs(C, c, *args) for c in range(C.NC)]
    res = run_bass_kernel_spmd(nc, in_maps, core_ids=list(range(C.NC)))
    out = np.stack([np.ascontiguousarray(res.results[c]["outT"].T)
                    for c in range(C.NC)], axis=0)
    return out.astype(np.float32)


# version 09
# speedup vs baseline: 1.0719x; 1.0719x over previous
import math
from collections import deque
from contextlib import ExitStack

import numpy as np
import ml_dtypes

import concourse.bass as bass
import concourse.mybir as mybir
from concourse.bass_utils import run_bass_kernel_spmd

F32 = mybir.dt.float32
BF16 = mybir.dt.bfloat16
I32 = mybir.dt.int32
AF = mybir.ActivationFunctionType
OP = mybir.AluOpType

SEMS = {}


class Cfg:
    def __init__(self, dm, l, din, nst, rdt, inter, ncores, cl, cb):
        self.DM, self.L, self.DIN, self.NST, self.RDT = dm, l, din, nst, rdt
        self.INTER, self.NC = inter, ncores
        self.CL, self.CB = cl, cb
        self.KC = 4
        self.NKD = dm // 128
        self.NJ = din // 128
        self.NOT = inter // 128
        self.NMT = dm // 128
        self.NLC = l // cl
        self.NCB = l // cb
        self.OTP = math.ceil(self.NOT / ncores)
        self.DMS = dm // ncores
        self.EDBL = rdt + 2 * nst
        self.EPS = 1e-5
        self.CBM = max(cb // 2, 64)
        self.CBLN = min(cb, 256)
        assert inter % 128 == 0 and self.DMS % 128 == 0 and cl % cb == 0
        assert cl % self.CBM == 0 and l % self.CBLN == 0


FULL = Cfg(dm=2048, l=2048, din=4096, nst=16, rdt=128, inter=5504, ncores=8,
           cl=1024, cb=512)
SMALL = Cfg(dm=256, l=512, din=512, nst=16, rdt=16, inter=256, ncores=2,
            cl=256, cb=128)


class Plan:
    ENGINES = ("sync", "vector", "scalar", "tensor", "gpsimd")

    def __init__(self):
        self.ops = {e: [] for e in self.ENGINES}
        self.emitted = {}
        self.scopes = {e: [] for e in self.ENGINES}
        self.cur_scope = "init"

    def set_scope(self, name):
        self.cur_scope = name

    def wait(self, eng, sem, val):
        if val is None or val <= 0:
            return
        assert val <= self.emitted.get(sem, 0), \
            f"plan deadlock: {eng} waits {sem}>={val} emitted={self.emitted.get(sem, 0)}"
        self.ops[eng].append(lambda e, s=sem, v=val: e.wait_ge(SEMS[s], v))
        self.scopes[eng].append(self.cur_scope)

    def op(self, eng, fn, inc=None, inc_n=1):
        self.scopes[eng].append(self.cur_scope)
        if inc is None:
            self.ops[eng].append(fn)
            return None
        self.emitted[inc] = self.emitted.get(inc, 0) + inc_n
        val = self.emitted[inc]
        self.ops[eng].append(
            lambda e, f=fn, s=inc, n=inc_n: f(e).then_inc(SEMS[s], n))
        return val

    def dma(self, eng, out, in_, q):
        return self.op(eng, lambda e, o=out, i=in_: e.dma_start(out=o, in_=i),
                       inc=q, inc_n=16)


class Arena:
    def __init__(self, tensor, nbytes, esize):
        self.t, self.nbytes, self.esize = tensor, nbytes, esize
        self.off = 0
        self.marks = []
        self.peak = 0

    def alloc(self, parts, elems):
        off = (self.off + self.esize * 16 - 1) // (self.esize * 16) * (self.esize * 16)
        self.off = off + elems * self.esize
        self.peak = max(self.peak, self.off)
        assert self.off <= self.nbytes, f"arena overflow {self.off}>{self.nbytes}"
        return self.t[0:parts, off // self.esize:off // self.esize + elems]

    def mark(self):
        self.marks.append(self.off)

    def release(self):
        self.off = self.marks.pop()


def build(cfg: Cfg, debug=False, DBG_TAPS=False, DBG_T=0):
    C = cfg
    NST, RDT, KC, EDBL = C.NST, C.RDT, C.KC, C.EDBL
    nc = bass.Bass("TRN2", target_bir_lowering=False)

    def ein(name, shape, dt):
        return nc.dram_tensor(name, list(shape), dt, kind="ExternalInput").ap()

    hT = ein("hT", (C.DM, C.L), F32)
    winT = ein("winT", (C.DM, 2 * C.DIN), F32)
    wxT = ein("wxT", (C.DIN, EDBL), F32)
    wdtT = ein("wdtT", (RDT, C.DIN), F32)
    wosl = ein("wosl", (C.DIN, C.DMS), F32)
    convw = ein("convw", (128, C.NJ * KC), F32)
    convb = ein("convb", (128, C.NJ), F32)
    dtb = ein("dtb", (128, C.NJ), F32)
    alog = ein("alog", (128, C.NJ * NST), F32)
    dvec = ein("dvec", (128, C.NJ), F32)
    lutv = ein("lutv", (1, 16), F32)
    gidx = ein("gidx", (C.DM, C.OTP * 128), I32)
    uidx = ein("uidx", (C.DM, C.OTP * 128), I32)
    didx = ein("didx", (C.INTER, C.DMS), I32)
    Ht = ein("Ht", (C.NJ, C.NJ, 128, 128), BF16)
    I128d = ein("I128", (128, 128), BF16)
    ones128 = ein("ones128", (128, 1), F32)

    outT = nc.dram_tensor("outT", [C.DM, C.L], F32, kind="ExternalOutput").ap()

    xnT_d = nc.dram_tensor("xnT_d", [C.DM, C.L], BF16).ap()
    u_d = nc.dram_tensor("u_d", [C.DIN, C.L], BF16).ap()
    sz_d = nc.dram_tensor("sz_d", [C.DIN, C.L], BF16).ap()
    bc_d = nc.dram_tensor("bc_d", [NST, 2 * C.L], BF16).ap()
    ln_d = nc.dram_tensor("ln_d", [2, C.L], F32).ap()
    y_d = nc.dram_tensor("y_d", [C.DIN, C.L], BF16).ap()
    gT_d = nc.dram_tensor("gT_d", [C.DM, C.L], BF16).ap()
    nmt_l = C.DMS // 128
    wg_sl = nc.dram_tensor("wg_sl", [C.OTP * C.NKD, 128, 128], BF16)
    wu_sl = nc.dram_tensor("wu_sl", [C.OTP * C.NKD, 128, 128], BF16)
    wd_sl = nc.dram_tensor("wd_sl", [nmt_l * C.NOT, 128, 128], BF16)
    w2_sl = nc.dram_tensor("w2_sl", [C.NJ, 128, C.DMS], BF16)
    wg_d = nc.dram_tensor("wg_d", [C.NC * C.OTP * C.NKD, 128, 128], BF16).ap()
    wu_d = nc.dram_tensor("wu_d", [C.NC * C.OTP * C.NKD, 128, 128], BF16).ap()
    wd_d = nc.dram_tensor("wd_d", [C.NC * nmt_l * C.NOT, 128, 128], BF16).ap()
    w2_d = nc.dram_tensor("w2_d", [C.NC * C.NJ, 128, C.DMS], BF16).ap()
    scd_d = nc.dram_tensor("scd_d", [2, 128, C.CL], F32).ap()
    car_d = nc.dram_tensor("car_d", [128, C.NJ * C.NST], F32).ap()
    scdB_d = nc.dram_tensor("scdB_d", [3, 128, C.CL], BF16).ap()

    pl = Plan()
    es = ExitStack()

    AFB = 40 * 1024
    ABB = 147 * 1024
    arenaF_t = es.enter_context(nc.sbuf_tensor("arenaF", [128, AFB // 4], F32))
    arenaB_t = es.enter_context(nc.sbuf_tensor("arenaB", [128, ABB // 2], BF16))
    arF = Arena(arenaF_t, AFB, 4)
    arB = Arena(arenaB_t, ABB, 2)

    def g_sb(name, shape, dt):
        return es.enter_context(nc.sbuf_tensor(name, list(shape), dt))

    i128 = g_sb("i128s", (128, 128), BF16)
    ones_s = g_sb("ones_s", (128, 1), F32)
    lutbc = g_sb("lutbc", (128, 16), F32)
    convw_s = g_sb("convw_s", (128, C.NJ * KC), F32)
    convb_s = g_sb("convb_s", (128, C.NJ), F32)
    dtb_s = g_sb("dtb_s", (128, C.NJ), F32)
    dvec_s = g_sb("dvec_s", (128, C.NJ), F32)
    negA = g_sb("negA", (128, C.NJ * NST), F32)
    wdt_bf = g_sb("wdt_bf", (RDT, C.DIN), BF16)
    dtr_bf = g_sb("dtr_bf", (RDT, C.L), BF16)
    carry = g_sb("carry", (128, C.NJ * NST), F32)
    diagD = g_sb("diagD", (128, 128), BF16)
    idx_i = g_sb("idx_i", (128, max(C.OTP * 128, C.DMS)), I32)

    pq = [es.enter_context(nc.psum_tensor(f"pq{i}", [128, 1024], F32))
          for i in range(4)]

    qs, qa, qb, qc2, eT, eV, eS, cc = "qs", "qa", "qb", "qc2", "eT", "eV", "eS", "cc"
    qy = "qy"
    cg = "cg"

    pl.set_scope("ph0")
    pl.dma("sync", i128[:], I128d[:], qs)
    pl.dma("sync", ones_s[:], ones128[:], qs)
    pl.dma("sync", lutbc[:], lutv[0:1, :].partition_broadcast(128), qs)
    pl.dma("sync", convw_s[:], convw[:], qs)
    pl.dma("sync", convb_s[:], convb[:], qs)
    pl.dma("sync", dtb_s[:], dtb[:], qs)
    pl.dma("sync", dvec_s[:], dvec[:], qs)
    v_consts = pl.dma("sync", negA[:], alog[:], qs)

    arF.mark()
    tmpA = arF.alloc(128, C.NJ * NST)
    wdt_f = arF.alloc(RDT, C.DIN)
    pl.wait("scalar", qs, v_consts)
    v_t = pl.op("scalar", lambda e: e.activation(out=tmpA, in_=negA[:], func=AF.Exp),
                inc=eS)
    pl.wait("vector", eS, v_t)
    v_negA = pl.op("vector", lambda e: e.tensor_scalar_mul(
        out=negA[:], in0=tmpA, scalar1=-1.0), inc=eV)
    v1 = pl.dma("sync", wdt_f, wdtT[:], qs)
    pl.wait("scalar", qs, v1)
    v_wdtc = pl.op("scalar", lambda e: e.copy(out=wdt_bf[:], in_=wdt_f), inc=eS)
    arF.release()

    arB.mark()
    matw = max(C.OTP * 128, C.DMS)
    idx_b = [arB.alloc(128, matw) for _ in range(2)]
    acc_bb = [arB.alloc(128, matw) for _ in range(2)]
    tmp_b = arB.alloc(128, matw)
    st = {"acc_store": {0: 0, 1: 0}, "idx_used": {0: 0, 1: 0}, "kt0": 0}

    def materialize(idx_dram, nrt, width, store_fn):
        for kt in range(nrt):
            gp = (st["kt0"] + kt) % 2
            pl.wait("sync", eV, max(st["idx_used"].values()))
            vq = pl.dma("sync", idx_i[0:128, 0:width],
                        idx_dram[kt * 128:(kt + 1) * 128, :], qa)
            pl.wait("vector", qa, vq)
            pl.wait("vector", qy, st["acc_store"][gp])
            pl.wait("vector", qs, v_consts)
            acc_b = acc_bb[gp]
            st["idx_used"][gp] = pl.op("vector", lambda e, w=width, gp=gp: e.tensor_copy(
                out=idx_b[gp][:, 0:w], in_=idx_i[0:128, 0:w]),
                inc=eV)
            pl.op("vector", lambda e, w=width, gp=gp, acc_b=acc_b: e.tensor_scalar(
                out=acc_b[:, 0:w], in0=idx_b[gp][:, 0:w], scalar1=0.0,
                scalar2=lutbc[:, 0:1], op0=OP.is_equal, op1=OP.mult))
            v = None
            for k in range(1, 16):
                pl.op("vector", lambda e, k=k, w=width, gp=gp: e.tensor_scalar(
                    out=tmp_b[:, 0:w], in0=idx_b[gp][:, 0:w], scalar1=float(k),
                    scalar2=lutbc[:, k:k + 1], op0=OP.is_equal, op1=OP.mult))
                v = pl.op("vector", lambda e, w=width, acc_b=acc_b: e.tensor_add(
                    out=acc_b[:, 0:w], in0=acc_b[:, 0:w], in1=tmp_b[:, 0:w]),
                    inc=eV if k == 15 else None)
            pl.wait("scalar", eV, v)
            last = 0
            for o_ap, i_ap in store_fn(kt, acc_b):
                last = pl.dma("scalar", o_ap, i_ap, qy)
            st["acc_store"][gp] = last
        st["kt0"] += nrt

    def gu_store(sl):
        def f(kt, acc_b):
            view = sl.ap().rearrange("(o k) p e -> o k p e", k=C.NKD)
            return [(view[o, kt, :, :], acc_b[:, o * 128:(o + 1) * 128])
                    for o in range(C.OTP)]
        return f

    def dn_store(kt, acc_b):
        view = wd_sl.ap().rearrange("(m k) p e -> m k p e", k=C.NOT)
        return [(view[m, kt, :, :], acc_b[:, m * 128:(m + 1) * 128])
                for m in range(nmt_l)]

    pl.set_scope("mat")
    materialize(gidx, C.NKD, C.OTP * 128, gu_store(wg_sl))
    materialize(uidx, C.NKD, C.OTP * 128, gu_store(wu_sl))
    materialize(didx, C.NOT, C.DMS, dn_store)
    v_mat_stores = pl.emitted.get(qy, 0)
    arB.release()

    pl.set_scope("w2")
    arB.mark()
    arF.mark()
    wo_b = arB.alloc(128, C.NJ * C.DMS)
    w2ev = arB.alloc(128, C.DMS)
    hsl = [arB.alloc(128, C.NJ * 128) for _ in range(2)]
    wo_f = arF.alloc(128, C.NJ * C.DMS)
    pl.wait("sync", eS, v_wdtc)
    pl.wait("sync", eV, v_negA)
    vq = pl.dma("sync", wo_f.rearrange("p (k e) -> p k e", e=C.DMS),
                wosl.rearrange("(k p) e -> p k e", p=128)[:], qs)
    pl.wait("vector", qs, vq)
    pl.wait("vector", qy, v_mat_stores)
    v_wocvt = pl.op("vector", lambda e: e.tensor_copy(out=wo_b, in_=wo_f), inc=eV)
    NSL = (C.DMS + 511) // 512
    v_w2ev, v_w2st = 0, 0
    vW2mm = {}
    for m in range(C.NJ):
        buf = hsl[m % 2]
        if m >= 2:
            pl.wait("sync", qy, v_w2st)
        vq = pl.dma("sync", buf.rearrange("p (k e) -> p k e", e=128),
                    Ht[m, :, :, :].rearrange("k p e -> p k e"), qs)
        pl.wait("tensor", qs, vq)
        pl.wait("tensor", eV, v_wocvt)
        pl.wait("tensor", eV, v_w2ev)
        vmm = None
        for kt in range(C.NJ):
            for si in range(NSL):
                s_, n_ = si * 512, min(512, C.DMS - si * 512)
                pw = pq[2][:, 0:512] if si == 0 else pq[2][:, 512:1024]
                vmm = pl.op("tensor", lambda e, b=buf, kt=kt, s_=s_, n_=n_, pw=pw: e.matmul(
                    out=pw[:, 0:n_], lhsT=b[:, kt * 128:(kt + 1) * 128],
                    rhs=wo_b[:, kt * C.DMS + s_:kt * C.DMS + s_ + n_],
                    start=(kt == 0), stop=(kt == C.NJ - 1)),
                    inc=eT if (kt == C.NJ - 1 and si == NSL - 1) else None)
        vW2mm[m] = vmm
        pl.wait("vector", eT, vmm)
        pl.wait("vector", qy, v_w2st)
        for si in range(NSL):
            s_, n_ = si * 512, min(512, C.DMS - si * 512)
            pw = pq[2][:, 0:512] if si == 0 else pq[2][:, 512:1024]
            v_w2ev = pl.op("vector", lambda e, s_=s_, n_=n_, pw=pw: e.tensor_copy(
                out=w2ev[:, s_:s_ + n_], in_=pw[:, 0:n_]),
                inc=eV if si == NSL - 1 else None)
        pl.wait("scalar", eV, v_w2ev)
        v_w2st = pl.dma("scalar", w2_sl.ap()[m, :, :], w2ev, qy)
    v_w2_stores = pl.emitted.get(qy, 0)
    arB.release()
    arF.release()

    pl.wait("gpsimd", qy, v_mat_stores)
    for sl, full in ((wg_sl, wg_d), (wu_sl, wu_d), (wd_sl, wd_d)):
        pl.op("gpsimd", lambda e, s=sl, f=full: e.collective_compute(
            "AllGather", OP.bypass, [list(range(C.NC))],
            [s.ap().opt()], [f.opt()]), inc=cc)
    pl.wait("gpsimd", qy, v_w2_stores)
    v_cc4 = pl.op("gpsimd", lambda e: e.collective_compute(
        "AllGather", OP.bypass, [list(range(C.NC))],
        [w2_sl.ap().opt()], [w2_d.opt()]), inc=cc)

    pl.set_scope("ln")
    arB.mark()
    xnT = arB.alloc(128, C.NKD * C.L)
    arF.mark()
    CBN = C.CBLN
    NCBN = C.L // CBN
    hTc = arF.alloc(128, C.NKD * CBN)
    sq2 = [arF.alloc(128, CBN) for _ in range(2)]
    r_mean = arF.alloc(1, CBN)
    r_t = arF.alloc(1, CBN)
    r_var = arF.alloc(1, CBN)
    r_rstd = arF.alloc(1, CBN)
    r_ms = arF.alloc(1, CBN)
    bc_rstd = arF.alloc(128, CBN)
    bc_ms = arF.alloc(128, CBN)
    pln0 = pq[3][0:1, 0:512]
    pln1 = pq[3][0:1, 512:1024]
    v_apply, v_rows = 0, 0
    vstat_hist = {}
    for cb in range(NCBN):
        lo = cb * CBN
        if cb > 0:
            pl.wait("sync", eV, v_apply)
        else:
            pl.wait("sync", eV, v_negA)
            pl.wait("sync", eV, v_wocvt)
            pl.wait("sync", eS, v_wdtc)
        vq = pl.dma("sync", hTc.rearrange("p (k l) -> p k l", l=CBN),
                    hT.rearrange("(k p) l -> p k l", p=128)[:, :, lo:lo + CBN], qs)
        pl.wait("tensor", qs, vq)
        pl.wait("tensor", eV, v_rows)
        for kt in range(C.NKD):
            pl.op("tensor", lambda e, kt=kt: e.matmul(
                out=pln0[:, 0:CBN], lhsT=ones_s[:],
                rhs=hTc[:, kt * CBN:(kt + 1) * CBN],
                start=(kt == 0), stop=(kt == C.NKD - 1)))
        pl.wait("scalar", qs, vq)
        vstat = None
        for kt in range(C.NKD):
            pl.wait("scalar", eT, vstat_hist.get((cb, kt - 2), 0))
            vsq = pl.op("scalar", lambda e, kt=kt: e.activation(
                out=sq2[kt % 2], in_=hTc[:, kt * CBN:(kt + 1) * CBN],
                func=AF.Square), inc=eS)
            pl.wait("tensor", eS, vsq)
            vstat = pl.op("tensor", lambda e, kt=kt: e.matmul(
                out=pln1[:, 0:CBN], lhsT=ones_s[:], rhs=sq2[kt % 2],
                start=(kt == 0), stop=(kt == C.NKD - 1)), inc=eT)
            vstat_hist[(cb, kt)] = vstat
        pl.wait("vector", eT, vstat)
        pl.op("vector", lambda e: e.tensor_scalar_mul(
            out=r_mean, in0=pln0[:, 0:CBN], scalar1=1.0 / C.DM))
        pl.op("vector", lambda e: e.tensor_scalar_mul(
            out=r_t, in0=pln1[:, 0:CBN], scalar1=1.0 / C.DM))
        pl.op("vector", lambda e: e.tensor_mul(out=r_var, in0=r_mean, in1=r_mean))
        pl.op("vector", lambda e: e.tensor_sub(out=r_var, in0=r_t, in1=r_var))
        v_rows = pl.op("vector", lambda e: e.tensor_scalar_add(
            out=r_var, in0=r_var, scalar1=C.EPS), inc=eV)
        pl.wait("scalar", eV, v_rows)
        v_ln = pl.op("scalar", lambda e: e.activation(out=r_rstd, in_=r_var,
                                                      func=AF.Ln), inc=eS)
        pl.wait("vector", eS, v_ln)
        v_half = pl.op("vector", lambda e: e.tensor_scalar_mul(
            out=r_rstd, in0=r_rstd, scalar1=-0.5), inc=eV)
        pl.wait("scalar", eV, v_half)
        v_rst = pl.op("scalar", lambda e: e.activation(out=r_rstd, in_=r_rstd,
                                                       func=AF.Exp), inc=eS)
        pl.wait("vector", eS, v_rst)
        v_msr = pl.op("vector", lambda e: e.tensor_mul(out=r_ms, in0=r_mean,
                                                       in1=r_rstd), inc=eV)
        pl.wait("sync", eV, v_msr)
        pl.dma("sync", ln_d[0:1, lo:lo + CBN], r_rstd, qs)
        v_rst_st = pl.dma("sync", ln_d[1:2, lo:lo + CBN], r_ms, qs)
        pl.wait("sync", qs, v_rst_st)
        pl.dma("sync", bc_rstd, ln_d[0:1, lo:lo + CBN].partition_broadcast(128), qs)
        vbc = pl.dma("sync", bc_ms, ln_d[1:2, lo:lo + CBN].partition_broadcast(128), qs)
        pl.wait("vector", qs, vbc)
        if cb == 0:
            pl.wait("vector", qy, v_w2_stores)
        for kt in range(C.NKD):
            pl.op("vector", lambda e, kt=kt: e.tensor_mul(
                out=hTc[:, kt * CBN:(kt + 1) * CBN],
                in0=hTc[:, kt * CBN:(kt + 1) * CBN], in1=bc_rstd))
            v_apply = pl.op("vector", lambda e, kt=kt, lo=lo: e.tensor_sub(
                out=xnT[:, kt * C.L + lo:kt * C.L + lo + CBN],
                in0=hTc[:, kt * CBN:(kt + 1) * CBN], in1=bc_ms),
                inc=eV if kt == C.NKD - 1 else None)
        pl.wait("scalar", eV, v_apply)
        pl.dma("scalar",
               xnT_d.rearrange("(k p) l -> p k l", p=128)[:, :, lo:lo + CBN],
               xnT.rearrange("p (k l) -> p k l", l=C.L)[:, :, lo:lo + CBN], qy)
    v_ln_done = v_apply
    v_xn_spill = pl.emitted.get(qy, 0)
    arF.release()

    pl.set_scope("m1")
    arF.mark()
    arB.mark()
    wsl_f = [arF.alloc(128, C.NKD * 128) for _ in range(4)]
    wsl_b = [arB.alloc(128, C.NKD * 128) for _ in range(4)]
    x_bf = [arB.alloc(128, C.L) for _ in range(2)]
    u_bf1 = [arB.alloc(128, C.L) for _ in range(2)]
    sz_bf1 = [arB.alloc(128, C.L) for _ in range(2)]
    diag4b = [arB.alloc(128, KC * 128) for _ in range(2)]
    LH = min(1024, C.L)
    NH = C.L // LH
    SL2 = min(512, LH)
    NSL2 = LH // SL2
    winT_r = winT.rearrange("(k p) e -> p k e", p=128)
    m1_marks = {}

    for j in range(C.NJ):
        par = j % 2
        px, pz = (pq[0], pq[1]) if par == 0 else (pq[2], pq[3])
        mk = m1_marks.get(j - 2, {})
        pl.wait("sync", eS, mk.get("wcvt"))
        if j == 0:
            pl.wait("sync", eV, v_ln_done)
        pl.dma("sync", wsl_f[2 * par], winT_r[:, :, j * 128:(j + 1) * 128], qs)
        v2 = pl.dma("sync", wsl_f[2 * par + 1],
                    winT_r[:, :, C.DIN + j * 128:C.DIN + (j + 1) * 128], qs)
        pl.wait("scalar", qs, v2)
        pl.wait("scalar", eT, mk.get("conv"))
        pl.op("scalar", lambda e, p=par: e.copy(out=wsl_b[2 * p], in_=wsl_f[2 * p]))
        v_wcvt = pl.op("scalar", lambda e, p=par: e.copy(
            out=wsl_b[2 * p + 1], in_=wsl_f[2 * p + 1]), inc=eS)
        pl.wait("scalar", eT, mk.get("conv"))
        pl.wait("scalar", qy, mk.get("spill", 0))
        pl.wait("tensor", eS, v_wcvt)
        pl.wait("tensor", eV, v_ln_done)
        v_xev = None
        for h2 in range(NH):
            vx = None
            for kt in range(C.NKD):
                for si in range(NSL2):
                    vx = pl.op("tensor",
                        lambda e, p=par, kt=kt, h2=h2, s_=si * SL2, px=px: e.matmul(
                            out=px[:, s_:s_ + SL2],
                            lhsT=wsl_b[2 * p][:, kt * 128:(kt + 1) * 128],
                            rhs=xnT[:, kt * C.L + h2 * LH + s_:
                                    kt * C.L + h2 * LH + s_ + SL2],
                            start=(kt == 0), stop=(kt == C.NKD - 1)),
                        inc=eT if (kt == C.NKD - 1 and si == NSL2 - 1) else None)
            pl.wait("scalar", eT, vx)
            v_xev = pl.op("scalar", lambda e, p=par, h2=h2, px=px: e.copy(
                out=x_bf[p][:, h2 * LH:(h2 + 1) * LH], in_=px[:, 0:LH]), inc=eS)
            if h2 + 1 < NH:
                pl.wait("tensor", eS, v_xev)
        v_szev = None
        for h2 in range(NH):
            vz = None
            for kt in range(C.NKD):
                for si in range(NSL2):
                    vz = pl.op("tensor",
                        lambda e, p=par, kt=kt, h2=h2, s_=si * SL2, pz=pz: e.matmul(
                            out=pz[:, s_:s_ + SL2],
                            lhsT=wsl_b[2 * p + 1][:, kt * 128:(kt + 1) * 128],
                            rhs=xnT[:, kt * C.L + h2 * LH + s_:
                                    kt * C.L + h2 * LH + s_ + SL2],
                            start=(kt == 0), stop=(kt == C.NKD - 1)),
                        inc=eT if (kt == C.NKD - 1 and si == NSL2 - 1) else None)
            pl.wait("scalar", eT, vz)
            v_szev = pl.op("scalar", lambda e, p=par, h2=h2, pz=pz: e.activation(
                out=sz_bf1[p][:, h2 * LH:(h2 + 1) * LH], in_=pz[:, 0:LH],
                func=AF.Silu), inc=eS)
            if h2 + 1 < NH:
                pl.wait("tensor", eS, v_szev)
        pl.wait("vector", qs, v_consts)
        if j >= 2:
            pl.wait("vector", eT, m1_marks[j - 2]["conv"])
        diag4 = diag4b[par]
        vd = None
        for k in range(KC):
            vd = pl.op("vector", lambda e, k=k, j=j, diag4=diag4: e.tensor_scalar(
                out=diag4[:, k * 128:(k + 1) * 128], in0=i128[:],
                scalar1=convw_s[:, j * KC + k:j * KC + k + 1], scalar2=None,
                op0=OP.mult), inc=eV if k == KC - 1 else None)
        pl.wait("tensor", eS, v_xev)
        pl.wait("tensor", eV, vd)
        v_uev = None
        vc = None
        CVS = min(512, LH)
        for h2 in range(NH):
            for q in range(LH // CVS):
                for k in range(KC):
                    sft = (KC - 1) - k
                    a = h2 * LH + q * CVS
                    o_lo = max(a, sft)
                    n_ = a + CVS - o_lo
                    vc = pl.op("tensor",
                        lambda e, p=par, k=k, o_lo=o_lo, n_=n_, sft=sft, px=px, h2=h2, diag4=diag4: e.matmul(
                            out=px[:, o_lo - h2 * LH:o_lo - h2 * LH + n_],
                            lhsT=diag4[:, k * 128:(k + 1) * 128],
                            rhs=x_bf[p][:, o_lo - sft:o_lo - sft + n_],
                            start=(k == 0), stop=(k == KC - 1)),
                        inc=eT if (k == KC - 1 and q == LH // CVS - 1) else None)
            pl.wait("scalar", eT, vc)
            v_uev = pl.op("scalar", lambda e, p=par, j=j, h2=h2, px=px: e.activation(
                out=u_bf1[p][:, h2 * LH:(h2 + 1) * LH], in_=px[:, 0:LH],
                func=AF.Silu, bias=convb_s[:, j:j + 1]), inc=eS)
            if h2 + 1 < NH:
                pl.wait("tensor", eS, v_uev)
        pl.wait("scalar", eS, v_uev)
        pl.dma("scalar", u_d[j * 128:(j + 1) * 128, :], u_bf1[par], qy)
        pl.wait("scalar", eS, v_szev)
        v_sp = pl.dma("scalar", sz_d[j * 128:(j + 1) * 128, :], sz_bf1[par], qy)
        m1_marks[j] = {"wcvt": v_wcvt, "conv": vc, "uev": v_uev, "spill": v_sp}
    v_m1_S = m1_marks[C.NJ - 1]["uev"]
    v_m1_sp = m1_marks[C.NJ - 1]["spill"]

    pl.set_scope("m1b")
    arB.release()
    arF.release()
    arB.mark()
    arF.mark()
    wx_f = arF.alloc(128, C.NJ * EDBL)
    wx_bf = arB.alloc(128, C.NJ * EDBL)
    ubk = [arB.alloc(128, C.L) for _ in range(2)]
    bcev = arB.alloc(2 * NST, C.L)
    pl.wait("sync", eS, m1_marks[C.NJ - 1]["wcvt"])
    vq = pl.dma("sync", wx_f.rearrange("p (j e) -> p j e", e=EDBL),
                wxT.rearrange("(j p) e -> p j e", p=128)[:], qs)
    pl.wait("scalar", qs, vq)
    pl.wait("scalar", eT, m1_marks[C.NJ - 1]["conv"])
    v_wxc = pl.op("scalar", lambda e: e.copy(out=wx_bf, in_=wx_f), inc=eS)
    NQ = C.L // min(512, C.L)
    SQ = C.L // NQ
    vmmJ = {}
    pl.wait("sync", qy, v_m1_sp)
    for j in range(C.NJ):
        if j >= 2:
            pl.wait("sync", eT, vmmJ[j - 2])
        v_uld = pl.dma("sync", ubk[j % 2], u_d[j * 128:(j + 1) * 128, :], qa)
        pl.wait("tensor", qa, v_uld)
        if j == 0:
            pl.wait("tensor", eS, v_wxc)
            pl.wait("tensor", eS, v_m1_S)
            pl.wait("tensor", eS, v_wdtc)
        vlast = None
        for qi in range(NQ):
            s_ = qi * SQ
            pdt = pq[0] if s_ < 1024 else pq[1]
            pl.op("tensor", lambda e, j=j, s_=s_, pdt=pdt: e.matmul(
                out=pdt[0:RDT, s_ % 1024:s_ % 1024 + SQ],
                lhsT=wx_bf[:, j * EDBL:j * EDBL + RDT],
                rhs=ubk[j % 2][:, s_:s_ + SQ],
                start=(j == 0), stop=(j == C.NJ - 1)))
        for qi in range(NQ):
            s_ = qi * SQ
            pbc = pq[2] if s_ < 1024 else pq[3]
            vlast = pl.op("tensor", lambda e, j=j, s_=s_, pbc=pbc: e.matmul(
                out=pbc[0:2 * NST, s_ % 1024:s_ % 1024 + SQ],
                lhsT=wx_bf[:, j * EDBL + RDT:(j + 1) * EDBL],
                rhs=ubk[j % 2][:, s_:s_ + SQ],
                start=(j == 0), stop=(j == C.NJ - 1)), inc=eT)
        vmmJ[j] = vlast
    v_xdbl = vmmJ[C.NJ - 1]
    pl.wait("scalar", eT, v_xdbl)
    for qi in range(NQ):
        s_ = qi * SQ
        pdt = pq[0] if s_ < 1024 else pq[1]
        pl.op("scalar", lambda e, s_=s_, pdt=pdt: e.copy(
            out=dtr_bf[:, s_:s_ + SQ], in_=pdt[0:RDT, s_ % 1024:s_ % 1024 + SQ]))
    v_bcev = None
    for qi in range(NQ):
        s_ = qi * SQ
        pbc = pq[2] if s_ < 1024 else pq[3]
        v_bcev = pl.op("scalar", lambda e, s_=s_, pbc=pbc: e.copy(
            out=bcev[:, s_:s_ + SQ], in_=pbc[0:2 * NST, s_ % 1024:s_ % 1024 + SQ]),
            inc=eS if qi == NQ - 1 else None)
    pl.wait("scalar", eS, v_bcev)
    pl.dma("scalar", bc_d[:, 0:C.L], bcev[0:NST, :], qy)
    v_bcst = pl.dma("scalar", bc_d[:, C.L:2 * C.L], bcev[NST:2 * NST, :], qy)
    arB.release()
    arF.release()
    arB.release()

    arF.mark()
    arB.mark()
    delta_f = arF.alloc(128, C.CL)
    ch1 = arF.alloc(128, min(512, C.CL))
    ch2 = arF.alloc(128, min(512, C.CL))
    dA2 = [arF.alloc(128, C.CL) for _ in range(2)]
    out_sb = [arF.alloc(128, C.CBM) for _ in range(2)]
    hT_t = [arF.alloc(128, C.CBM) for _ in range(2)]
    du_b = arB.alloc(128, C.CL)
    dbu2 = [arB.alloc(128, C.CL) for _ in range(2)]
    h2 = [arB.alloc(128, C.CL) for _ in range(2)]
    ytmp = [arB.alloc(128, C.CL) for _ in range(4)]
    bcr = [arB.alloc(128, 2 * C.CL) for _ in range(3)]
    u_sc = [arB.alloc(128, C.CL) for _ in range(2)]
    sz_sc = [arB.alloc(128, C.CL) for _ in range(2)]
    ybf = [arB.alloc(128, C.CL) for _ in range(2)]
    xn_fc = arB.alloc(128, C.NKD * C.CB)
    h_ffn = arB.alloc(128, C.NOT * C.CB)
    sg_b = arB.alloc(128, C.CB)
    th_f = arF.alloc(128, C.CB)
    th_a = arF.alloc(128, C.CB)
    gwsl = [arB.alloc(128, C.NKD * 128) for _ in range(2)]
    uwsl = [arB.alloc(128, C.NKD * 128) for _ in range(2)]
    NOTH = (C.NOT + 3) // 4
    dwsl = arB.alloc(128, NOTH * 128)
    w2sl_b = arB.alloc(128, C.NJ * 128)
    ysl = arB.alloc(128, C.NJ * C.CBM)
    gt_t = [arB.alloc(128, C.CBM) for _ in range(2)]

    pY = pq[0]
    pDl = pq[1][:, 0:512]
    pFg2 = [pq[2][:, 0:512], pq[2][:, 512:1024]]
    pFu2 = [pq[1][:, 512:1024], pq[3][:, 512:1024]]
    pM = pq[3][:, 0:512]

    CBL = C.CL // C.CB

    fstate = {
        "gt_store": {}, "yspill": {}, "yslot": {},
        "vhm_last": 0, "vsg_last": 0, "vdownev": 0, "vdown_last": 0,
        "vcomb": 0, "vout_st": {}, "vgt_read": 0,
        "vxn_free": 0, "vslab": {}, "vdsl": 0, "vw2sl": 0, "vysl": 0,
    }

    def gen_ffn(fc):
        pl.set_scope("ffn")
        lo = fc * C.CB
        pl.wait("sync", qy, v_xn_spill)
        pl.wait("sync", eT, fstate["vxn_free"])
        vxn = pl.dma(
            "sync", xn_fc.rearrange("p (k l) -> p k l", l=C.CB),
            xnT_d.rearrange("(k p) l -> p k l", p=128)[:, :, lo:lo + C.CB],
            qa)
        yield
        vg = {}
        vu = {}
        vth = {}

        def emit_mms(g):
            gp = g % 2
            c_, ot_l = divmod(g, C.OTP)
            base = (c_ * C.OTP + ot_l) * C.NKD
            pl.wait("sync", cc, 3)
            if g >= 2:
                pl.wait("sync", eT, vu[g - 2])
            pl.dma("sync", gwsl[gp].rearrange("p (t e) -> p t e", e=128),
                   wg_d.rearrange("t p e -> p t e")[:, base:base + C.NKD, :],
                   qa)
            return pl.dma("sync", uwsl[gp].rearrange("p (t e) -> p t e", e=128),
                          wu_d.rearrange("t p e -> p t e")[:, base:base + C.NKD, :],
                          qa)

        def emit_gu(g, vsl):
            gp = g % 2
            pl.wait("tensor", qa, vsl)
            pl.wait("tensor", qa, vxn)
            pl.wait("tensor", eV, fstate["vsg_last"] if g < 2 else vhm_g.get(g - 2, 0))
            if g == 0 and fc == 0:
                pl.wait("tensor", eS, v_bcev)
            for kt in range(C.NKD):
                vg[g] = pl.op("tensor", lambda e, kt=kt, gp=gp: e.matmul(
                    out=pFg2[gp][:, 0:C.CB], lhsT=gwsl[gp][:, kt * 128:(kt + 1) * 128],
                    rhs=xn_fc[:, kt * C.CB:(kt + 1) * C.CB],
                    start=(kt == 0), stop=(kt == C.NKD - 1)),
                    inc=eT if kt == C.NKD - 1 else None)
            for kt in range(C.NKD):
                vu[g] = pl.op("tensor", lambda e, kt=kt, gp=gp: e.matmul(
                    out=pFu2[gp][:, 0:C.CB], lhsT=uwsl[gp][:, kt * 128:(kt + 1) * 128],
                    rhs=xn_fc[:, kt * C.CB:(kt + 1) * C.CB],
                    start=(kt == 0), stop=(kt == C.NKD - 1)),
                    inc=eT if kt == C.NKD - 1 else None)

        def emit_act(g):
            gp = g % 2
            pl.wait("scalar", eT, vg[g])
            pl.wait("scalar", eV, fstate["vsg_last"])
            vth[g] = pl.op("scalar", lambda e, gp=gp: e.activation(
                out=th_f, in_=pFg2[gp][:, 0:C.CB], func=AF.Tanh, scale=0.5), inc=eS)

        vhm_g = {}

        def emit_hm(g):
            gp = g % 2
            pl.wait("vector", eS, vth[g])
            pl.wait("vector", eV, fstate["vhm_last"])
            if g == 0:
                pl.wait("vector", qy, fstate["gt_store"].get((fc - 1, C.NMT - 1), 0))
            pl.op("vector", lambda e: e.tensor_scalar(
                out=th_a, in0=th_f, scalar1=1.0, scalar2=0.5,
                op0=OP.add, op1=OP.mult))
            vsg = pl.op("vector", lambda e, gp=gp: e.tensor_mul(
                out=sg_b, in0=th_a, in1=pFg2[gp][:, 0:C.CB]), inc=eV)
            fstate["vsg_last"] = vsg
            pl.wait("vector", eT, vu[g])
            if g == 0:
                pl.wait("vector", eT, fstate["vdown_last"])
            vhm = pl.op("vector", lambda e, g=g, gp=gp: e.tensor_mul(
                out=h_ffn[:, g * C.CB:(g + 1) * C.CB], in0=sg_b,
                in1=pFu2[gp][:, 0:C.CB]), inc=eV)
            vhm_g[g] = vhm
            fstate["vhm_last"] = vhm

        vsl_next = emit_mms(0)
        emit_gu(0, vsl_next)
        yield
        for g in range(1, C.NOT):
            vsl_next = emit_mms(g)
            yield
            emit_gu(g, vsl_next)
            yield
            emit_act(g - 1)
            emit_hm(g - 1)
            yield
        emit_act(C.NOT - 1)
        emit_hm(C.NOT - 1)
        yield
        fstate["vxn_free"] = vu[C.NOT - 1]
        vhm_all = fstate["vhm_last"]
        for mt in range(C.NMT):
            c_, mt_l = divmod(mt, nmt_l)
            base = (c_ * nmt_l + mt_l) * C.NOT
            vdn = None
            for half in range(4):
                k0 = half * NOTH
                k1 = min(C.NOT, k0 + NOTH)
                if k1 <= k0:
                    continue
                pl.wait("sync", eT, fstate["vdsl"])
                vdsl = pl.dma("sync", dwsl.rearrange("p (t e) -> p t e", e=128)
                              [:, 0:k1 - k0, :],
                              wd_d.rearrange("t p e -> p t e")
                              [:, base + k0:base + k1, :], qa)
                yield
                pl.wait("tensor", qa, vdsl)
                pl.wait("tensor", eV, vhm_all)
                if half == 0:
                    pl.wait("tensor", eV, fstate["vdownev"])
                    pl.wait("tensor", eV, fstate["vsg_last"])
                for ko in range(k0, k1):
                    vdn = pl.op("tensor", lambda e, ko=ko, k0=k0: e.matmul(
                        out=pFg2[0][:, 0:C.CB],
                        lhsT=dwsl[:, (ko - k0) * 128:(ko - k0 + 1) * 128],
                        rhs=h_ffn[:, ko * C.CB:(ko + 1) * C.CB],
                        start=(ko == 0), stop=(ko == C.NOT - 1)),
                        inc=eT if ko == k1 - 1 else None)
                    if ko % 8 == 7:
                        yield
                fstate["vdsl"] = vdn
            if mt == C.NMT - 1:
                fstate["vdown_last"] = vdn
            pl.wait("scalar", eT, vdn)
            pl.wait("scalar", eV, fstate["vsg_last"])
            vth2 = pl.op("scalar", lambda e: e.activation(
                out=th_f, in_=pFg2[0][:, 0:C.CB], func=AF.Tanh, scale=0.5), inc=eS)
            pl.wait("vector", eS, vth2)
            pl.wait("vector", eV, fstate["vhm_last"])
            pl.wait("vector", qy, fstate["gt_store"].get((fc, mt - 1), 0))
            pl.op("vector", lambda e: e.tensor_scalar(
                out=th_a, in0=th_f, scalar1=1.0, scalar2=0.5,
                op0=OP.add, op1=OP.mult))
            vgt = pl.op("vector", lambda e: e.tensor_mul(
                out=sg_b, in0=th_a, in1=pFg2[0][:, 0:C.CB]), inc=eV)
            fstate["vdownev"] = vgt
            fstate["vsg_last"] = vgt
            pl.wait("scalar", eV, vgt)
            fstate["gt_store"][(fc, mt)] = pl.dma(
                "scalar", gT_d[mt * 128:(mt + 1) * 128, lo:lo + C.CB], sg_b, qy)
            yield

    def gen_m4(s):
        pl.set_scope("m4")
        lc = (s * C.CBM) // C.CL
        lo = s * C.CBM
        CBM = C.CBM
        pl.wait("sync", qy, fstate["yspill"][lc])
        pl.wait("sync", eT, fstate["vysl"])
        vy = pl.dma("sync", ysl.rearrange("p (k l) -> p k l", l=CBM),
                    y_d.rearrange("(k p) l -> p k l", p=128)[:, :, lo:lo + CBM],
                    qa)
        yield
        fcg = lo // C.CB
        for mt in range(C.NMT):
            mp = mt % 2
            c_ = (mt * 128) // C.DMS
            colo = (mt * 128) % C.DMS
            pl.wait("sync", cc, v_cc4)
            pl.wait("sync", eT, fstate["vw2sl"])
            vw2 = pl.dma("sync", w2sl_b.rearrange("p (k e) -> p k e", e=128),
                         w2_d.rearrange("(r k) p e -> r p k e", k=C.NJ)
                         [c_, :, :, colo:colo + 128], qa)
            pl.wait("sync", qy, fstate["gt_store"][(fcg, mt)])
            pl.wait("sync", eV, fstate["vgt_read"])
            pl.wait("sync", qy, fstate["vout_st"].get(mp, 0))
            pl.dma("sync", gt_t[mp], gT_d[mt * 128:(mt + 1) * 128, lo:lo + CBM], qs)
            vht = pl.dma("sync", hT_t[mp],
                         hT[mt * 128:(mt + 1) * 128, lo:lo + CBM], qs)
            yield
            pl.wait("tensor", qa, vy)
            pl.wait("tensor", qa, vw2)
            pl.wait("tensor", eV, fstate["vcomb"])
            vmm = None
            for kt in range(C.NJ):
                vmm = pl.op("tensor", lambda e, kt=kt: e.matmul(
                    out=pM[:, 0:CBM], lhsT=w2sl_b[:, kt * 128:(kt + 1) * 128],
                    rhs=ysl[:, kt * CBM:(kt + 1) * CBM],
                    start=(kt == 0), stop=(kt == C.NJ - 1)),
                    inc=eT if kt == C.NJ - 1 else None)
                if kt % 8 == 7:
                    yield
            fstate["vw2sl"] = vmm
            fstate["vysl"] = vmm
            pl.wait("vector", eT, vmm)
            pl.wait("vector", qs, vht)
            pl.wait("vector", qy, fstate["vout_st"].get(mp, 0))
            pl.op("vector", lambda e, mp=mp: e.tensor_mul(
                out=gt_t[mp], in0=gt_t[mp], in1=pM[:, 0:CBM]))
            vcb = pl.op("vector", lambda e, mp=mp: e.tensor_add(
                out=out_sb[mp], in0=gt_t[mp], in1=hT_t[mp]), inc=eV)
            fstate["vcomb"] = vcb
            fstate["vgt_read"] = vcb
            pl.wait("scalar", eV, vcb)
            fstate["vout_st"][mp] = pl.dma(
                "scalar", outT[mt * 128:(mt + 1) * 128, lo:lo + CBM],
                out_sb[mp], qy)
            yield

    genq = deque()
    for fc in range(C.NCB):
        genq.append(gen_ffn(fc))

    def pump(k=1):
        for _ in range(k):
            while genq:
                try:
                    next(genq[0])
                    return
                except StopIteration:
                    genq.popleft()
            return

    vstep = {}
    v_scan = {}
    v_dbu = {}
    v_ym = {}
    v_car = {}
    vyadd = {}
    vtD = {}
    vgate = {}
    vspU = {}
    v_bc = {}
    v_du = {}
    v_dAw = {}
    NTOT = C.NLC * C.NJ

    def emit_bcast(st_):
        if st_ >= NTOT * NST:
            return
        lc2, r = divmod(st_, C.NJ * NST)
        j2, n2 = divmod(r, NST)
        par = st_ % 3
        pl.wait("sync", qy, v_bcst)
        pl.wait("sync", eV, vstep.get(st_ - 3, 0))
        lo2 = lc2 * C.CL
        src = bc_d[n2:n2 + 1, :].rearrange("p (s l) -> p s l", s=2)[:, :, lo2:lo2 + C.CL]
        v_bc[st_] = pl.dma("sync",
                           bcr[par].rearrange("p (s l) -> p s l", s=2),
                           src.partition_broadcast(128), qb)

    def emit_uld(t2):
        if t2 >= NTOT:
            return
        lc2, j2 = divmod(t2, C.NJ)
        par = t2 % 2
        pl.wait("sync", qy, v_m1_sp)
        pl.wait("sync", eV, v_du.get(t2 - 2, 0))
        pl.wait("sync", eT, vtD.get(t2 - 2, 0))
        pl.wait("sync", eV, vgate.get(t2 - 2, 0))
        lo2 = lc2 * C.CL
        pl.dma("sync", u_sc[par], u_d[j2 * 128:(j2 + 1) * 128, lo2:lo2 + C.CL], qa)
        vspU[t2] = pl.dma("sync", sz_sc[par],
                          sz_d[j2 * 128:(j2 + 1) * 128, lo2:lo2 + C.CL], qa)

    for _st in range(3):
        emit_bcast(_st)
    emit_uld(0)
    emit_uld(1)

    for lc in range(C.NLC):
        lo = lc * C.CL
        for j in range(C.NJ):
            t = lc * C.NJ + j
            pl.set_scope(f"scan{lc}")
            par = t % 2
            NQ2 = C.CL // min(512, C.CL)
            SQ2 = C.CL // NQ2
            vsp = None
            for si in range(NQ2):
                s_ = si * SQ2
                pl.wait("tensor", eS, v_dAw.get((t, si - 1), v_dAw.get((t - 1, NQ2 - 1), 0)))
                if t == 0 and si == 0:
                    pl.wait("tensor", eS, v_bcev)
                vdt = pl.op("tensor", lambda e, j=j, lo=lo, s_=s_: e.matmul(
                    out=pDl[:, 0:SQ2],
                    lhsT=wdt_bf[:, j * 128:(j + 1) * 128],
                    rhs=dtr_bf[:, lo + s_:lo + s_ + SQ2],
                    start=True, stop=True), inc=eT)
                pl.wait("scalar", eT, vdt)
                if si == 0:
                    pl.wait("scalar", eV, v_du.get(t - 1, 0))
                pl.op("scalar", lambda e, j=j, s_=s_: e.activation(
                    out=delta_f[:, s_:s_ + SQ2], in_=pDl[:, 0:SQ2],
                    func=AF.Identity, bias=dtb_s[:, j:j + 1]))
                if si > 0:
                    pl.wait("scalar", eV, vchadd)
                v_dAw[(t, si)] = pl.op("scalar", lambda e, s_=s_: e.activation(
                    out=ch1[:, 0:SQ2], in_=delta_f[:, s_:s_ + SQ2],
                    func=AF.Abs), inc=eS)
                pl.op("scalar", lambda e: e.activation(
                    out=ch2[:, 0:SQ2], in_=ch1[:, 0:SQ2],
                    func=AF.Exp, scale=-1.0))
                pl.op("scalar", lambda e: e.activation(
                    out=ch1[:, 0:SQ2], in_=ch2[:, 0:SQ2],
                    func=AF.Ln, bias=1.0))
                vsp = pl.op("scalar", lambda e, s_=s_: e.activation(
                    out=ch2[:, 0:SQ2], in_=delta_f[:, s_:s_ + SQ2],
                    func=AF.Relu), inc=eS)
                pl.wait("vector", eS, vsp)
                vchadd = pl.op("vector", lambda e, s_=s_: e.tensor_add(
                    out=delta_f[:, s_:s_ + SQ2], in0=ch1[:, 0:SQ2],
                    in1=ch2[:, 0:SQ2]), inc=eV)
            pl.wait("vector", qa, vspU[t])
            pl.wait("vector", cg, v_dbu.get(t * NST - 1, 0))
            v_du[t] = pl.op("vector", lambda e, par=par: e.tensor_mul(
                out=du_b, in0=delta_f, in1=u_sc[par]), inc=eV)
            pl.wait("vector", eT, vtD.get(t - 1, 0))
            vdg = pl.op("vector", lambda e, j=j: e.tensor_scalar(
                out=diagD[:], in0=i128[:],
                scalar1=dvec_s[:, j:j + 1], scalar2=None, op0=OP.mult), inc=eV)
            for n in range(NST):
                st_ = t * NST + n
                sp = st_ % 2
                spy = st_ % 4
                sp4 = st_ % 3
                col = j * NST + n
                pl.wait("scalar", eV, v_du[t])
                pl.wait("scalar", eV, v_scan.get(st_ - 2, 0))
                vda = pl.op("scalar", lambda e, j=j, n=n, sp=sp: e.activation(
                    out=dA2[sp], in_=delta_f, func=AF.Exp,
                    scale=negA[:, (j * NST + n):(j * NST + n) + 1]), inc=eS)
                pl.wait("gpsimd", eV, v_du[t])
                pl.wait("gpsimd", qb, v_bc[st_])
                pl.wait("gpsimd", eV, v_scan.get(st_ - 2, 0))
                v_dbu[st_] = pl.op("gpsimd", lambda e, sp=sp, sp4=sp4: e.tensor_tensor(
                    out=dbu2[sp], in0=du_b, in1=bcr[sp4][:, 0:C.CL],
                    op=OP.mult), inc=cg)
                pl.wait("vector", eS, vda)
                pl.wait("vector", cg, v_dbu[st_])
                pl.wait("vector", eT, vyadd.get(st_ - 4, 0))
                if lc == 0:
                    v_scan[st_] = pl.op("vector", lambda e, sp=sp: e.tensor_tensor_scan(
                        out=h2[sp], data0=dA2[sp], data1=dbu2[sp], initial=0.0,
                        op0=OP.mult, op1=OP.add), inc=eV)
                else:
                    v_scan[st_] = pl.op("vector", lambda e, sp=sp, col=col: e.tensor_tensor_scan(
                        out=h2[sp], data0=dA2[sp], data1=dbu2[sp],
                        initial=carry[:, col:col + 1], op0=OP.mult, op1=OP.add),
                        inc=eV)
                vstep[st_] = pl.op("vector", lambda e, sp=sp, spy=spy, sp4=sp4: e.tensor_mul(
                    out=ytmp[spy], in0=h2[sp], in1=bcr[sp4][:, C.CL:2 * C.CL]),
                    inc=eV)
                if lc + 1 < C.NLC:
                    pl.op("vector", lambda e, sp=sp, col=col: e.tensor_copy(
                        out=carry[:, col:col + 1], in_=h2[sp][:, C.CL - 1:C.CL]))
                if DBG_TAPS and t == DBG_T and n == 0:
                    pl.wait("sync", eV, vstep[st_])
                    pl.dma("sync", scd_d[0], delta_f, qs)
                    pl.dma("sync", scd_d[1], dA2[sp], qs)
                    pl.dma("sync", scdB_d[0], dbu2[sp], qs)
                    pl.dma("sync", scdB_d[1], h2[sp], qs)
                    pl.dma("sync", car_d[:], carry[:], qs)
                    vdbg2 = pl.dma("sync", scdB_d[2], ytmp[spy], qs)
                    pl.wait("vector", qs, vdbg2)
                pl.wait("tensor", eV, vstep[st_])
                if n == 0:
                    pl.wait("tensor", eV, vgate.get(t - 1, 0))
                for ci in range(CBL):
                    vyadd[st_] = pl.op("tensor", lambda e, spy=spy, ci=ci, n=n: e.matmul(
                        out=pY[:, ci * C.CB:(ci + 1) * C.CB], lhsT=i128[:],
                        rhs=ytmp[spy][:, ci * C.CB:(ci + 1) * C.CB],
                        start=(n == 0), stop=False),
                        inc=eT if ci == CBL - 1 else None)
                emit_bcast(st_ + 3)
                sc_ = pl.cur_scope
                pump(1)
                pump(1)
                pl.set_scope(sc_)
            pl.wait("tensor", eV, vdg)
            vD = None
            for ci in range(CBL):
                vD = pl.op("tensor", lambda e, par=par, ci=ci: e.matmul(
                    out=pY[:, ci * C.CB:(ci + 1) * C.CB], lhsT=diagD[:],
                    rhs=u_sc[par][:, ci * C.CB:(ci + 1) * C.CB],
                    start=False, stop=True),
                    inc=eT if ci == CBL - 1 else None)
            vtD[t] = vD
            pl.wait("vector", eT, vD)
            pl.wait("vector", qa, vspU[t])
            rp = t % 2
            pl.wait("vector", qy, fstate["yslot"].get(t - 2, 0))
            vgate[t] = pl.op("vector", lambda e, par=par, rp=rp: e.tensor_mul(
                out=ybf[rp], in0=pY[:, 0:C.CL], in1=sz_sc[par]), inc=eV)
            pl.wait("scalar", eV, vgate[t])
            vys = pl.dma("scalar", y_d[j * 128:(j + 1) * 128, lo:lo + C.CL],
                         ybf[rp], qy)
            fstate["yslot"][t] = vys
            fstate["yspill"][lc] = vys
            emit_uld(t + 2)
            sc_ = pl.cur_scope
            pump(1)
            pl.set_scope(sc_)
        for ci in range(C.CL // C.CBM):
            genq.append(gen_m4(lc * (C.CL // C.CBM) + ci))

    while genq:
        try:
            next(genq[0])
        except StopIteration:
            genq.popleft()

    if debug:
        dbgs = [("w2sl_src", w2_sl.ap(), [C.NJ, 128, C.DMS], BF16),
                ("car_d", car_d, [128, C.NJ * C.NST], F32),
                ("scd_d", scd_d, [2, 128, C.CL], F32),
                ("scdB_d", scdB_d, [3, 128, C.CL], BF16),
                ("xnT_d", xnT_d, [C.DM, C.L], BF16),
                ("u_d", u_d, [C.DIN, C.L], BF16),
                ("sz_d", sz_d, [C.DIN, C.L], BF16),
                ("bc_d", bc_d, [2 * NST, C.L], BF16),
                ("y_d", y_d, [C.DIN, C.L], BF16),
                ("gT_d", gT_d, [C.DM, C.L], BF16),
                ("w2_d", w2_d, [C.NC * C.NJ, 128, C.DMS], BF16),
                ("wg_d", wg_d, [C.NC * C.OTP * C.NKD, 128, 128], BF16)]
        pl.wait("sync", qa, pl.emitted.get(qa, 0))
        pl.wait("sync", qs, pl.emitted.get(qs, 0))
        pl.wait("sync", qb, pl.emitted.get(qb, 0))
        pl.wait("sync", qc2, pl.emitted.get(qc2, 0))
        for nm, src, shp, dt in dbgs:
            dst = nc.dram_tensor(nm + "_dbg", shp, dt, kind="ExternalOutput").ap()
            pl.dma("sync", dst[:], src[:], qs)
    for q in (qa, qy, qs, qb, qc2, cc):
        pl.wait("sync", q, pl.emitted.get(q, 0))

    for name in sorted(pl.emitted.keys()):
        SEMS[name] = es.enter_context(nc.semaphore(name))
    def replay(e, eng):
        cur = None
        ctx = None
        for f, sc in zip(pl.ops[eng], pl.scopes[eng]):
            if sc != cur:
                if ctx is not None:
                    ctx.__exit__(None, None, None)
                ctx = nc.named_scope(f"{sc}_{eng[0]}")
                ctx.__enter__()
                cur = sc
            f(e)
        if ctx is not None:
            ctx.__exit__(None, None, None)

    with nc.Block() as block:
        @block.sync
        def _(e):
            replay(e, "sync")

        @block.vector
        def _(e):
            replay(e, "vector")

        @block.scalar
        def _(e):
            replay(e, "scalar")

        @block.tensor
        def _(e):
            replay(e, "tensor")

        @block.gpsimd
        def _(e):
            replay(e, "gpsimd")
    es.close()
    return nc


def _hadamard(n):
    H = np.array([[1.0]], dtype=np.float32)
    while H.shape[0] < n:
        H = np.block([[H, H], [H, -H]])
    return H


def host_inputs(cfg: Cfg, core, hidden, W_in, conv_w, conv_b, W_x, W_dt,
                dt_bias, A_log, D, W_out, lut, gate_idx, up_idx, down_idx):
    C = cfg
    bf = ml_dtypes.bfloat16
    f32 = np.float32
    H = (_hadamard(C.DIN) / math.sqrt(C.DIN)).astype(f32)
    Ht = np.ascontiguousarray(
        H.reshape(C.NJ, 128, C.NJ, 128).transpose(2, 0, 1, 3)).astype(bf)
    gi = np.zeros((C.DM, C.OTP * C.NC * 128), np.int32)
    gi[:, 0:C.INTER] = gate_idx.T
    ui = np.zeros((C.DM, C.OTP * C.NC * 128), np.int32)
    ui[:, 0:C.INTER] = up_idx.T
    di = np.ascontiguousarray(down_idx.T.astype(np.int32))
    o0 = core * C.OTP * 128
    m0 = core * C.DMS
    return {
        "hT": np.ascontiguousarray(hidden[core].T).astype(f32),
        "winT": np.ascontiguousarray(W_in.T).astype(f32),
        "wxT": np.ascontiguousarray(W_x.T).astype(f32),
        "wdtT": np.ascontiguousarray(W_dt.T).astype(f32),
        "wosl": np.ascontiguousarray(W_out.T[:, m0:m0 + C.DMS]).astype(f32),
        "convw": np.ascontiguousarray(
            conv_w.reshape(C.NJ, 128, C.KC).transpose(1, 0, 2)
            .reshape(128, C.NJ * C.KC)).astype(f32),
        "convb": np.ascontiguousarray(
            conv_b.reshape(C.NJ, 128).T).astype(f32),
        "dtb": np.ascontiguousarray(
            dt_bias.reshape(C.NJ, 128).T).astype(f32),
        "alog": np.ascontiguousarray(
            A_log.reshape(C.NJ, 128, C.NST).transpose(1, 0, 2)
            .reshape(128, C.NJ * C.NST)).astype(f32),
        "dvec": np.ascontiguousarray(
            D.reshape(C.NJ, 128).T).astype(f32),
        "lutv": np.ascontiguousarray(lut.reshape(1, -1)).astype(f32),
        "gidx": np.ascontiguousarray(gi[:, o0:o0 + C.OTP * 128]),
        "uidx": np.ascontiguousarray(ui[:, o0:o0 + C.OTP * 128]),
        "didx": np.ascontiguousarray(di[:, m0:m0 + C.DMS]),
        "Ht": Ht,
        "I128": np.eye(128, dtype=bf),
        "ones128": np.ones((128, 1), np.float32),
    }


def kernel(L0_attn_hidden_states, L0_attn_W_in, L0_attn_conv_w, L0_attn_conv_b,
           L0_attn_W_x, L0_attn_W_dt, L0_attn_dt_bias, L0_attn_A_log,
           L0_attn_D, L0_attn_W_out, L0_ffn_lut, L0_ffn_gate_idx,
           L0_ffn_up_idx, L0_ffn_down_idx):
    C = FULL
    args = (np.asarray(L0_attn_hidden_states), np.asarray(L0_attn_W_in),
            np.asarray(L0_attn_conv_w), np.asarray(L0_attn_conv_b),
            np.asarray(L0_attn_W_x), np.asarray(L0_attn_W_dt),
            np.asarray(L0_attn_dt_bias), np.asarray(L0_attn_A_log),
            np.asarray(L0_attn_D), np.asarray(L0_attn_W_out),
            np.asarray(L0_ffn_lut), np.asarray(L0_ffn_gate_idx),
            np.asarray(L0_ffn_up_idx), np.asarray(L0_ffn_down_idx))
    nc = build(C)
    in_maps = [host_inputs(C, c, *args) for c in range(C.NC)]
    res = run_bass_kernel_spmd(nc, in_maps, core_ids=list(range(C.NC)))
    out = np.stack([np.ascontiguousarray(res.results[c]["outT"].T)
                    for c in range(C.NC)], axis=0)
    return out.astype(np.float32)


# version 10
# speedup vs baseline: 1.1289x; 1.0532x over previous
import math
from collections import deque
from contextlib import ExitStack

import numpy as np
import ml_dtypes

import concourse.bass as bass
import concourse.mybir as mybir
from concourse.bass_utils import run_bass_kernel_spmd

F32 = mybir.dt.float32
BF16 = mybir.dt.bfloat16
I32 = mybir.dt.int32
AF = mybir.ActivationFunctionType
OP = mybir.AluOpType

SEMS = {}


class Cfg:
    def __init__(self, dm, l, din, nst, rdt, inter, ncores, cl, cb):
        self.DM, self.L, self.DIN, self.NST, self.RDT = dm, l, din, nst, rdt
        self.INTER, self.NC = inter, ncores
        self.CL, self.CB = cl, cb
        self.KC = 4
        self.NKD = dm // 128
        self.NJ = din // 128
        self.NOT = inter // 128
        self.NMT = dm // 128
        self.NLC = l // cl
        self.NCB = l // cb
        self.OTP = math.ceil(self.NOT / ncores)
        self.DMS = dm // ncores
        self.EDBL = rdt + 2 * nst
        self.EPS = 1e-5
        self.CBM = max(cb // 2, 64)
        self.CBLN = min(cb, 256)
        assert inter % 128 == 0 and self.DMS % 128 == 0 and cl % cb == 0
        assert cl % self.CBM == 0 and l % self.CBLN == 0


FULL = Cfg(dm=2048, l=2048, din=4096, nst=16, rdt=128, inter=5504, ncores=8,
           cl=1024, cb=512)
SMALL = Cfg(dm=256, l=512, din=512, nst=16, rdt=16, inter=256, ncores=2,
            cl=256, cb=128)


class Plan:
    ENGINES = ("sync", "vector", "scalar", "tensor", "gpsimd")

    def __init__(self):
        self.ops = {e: [] for e in self.ENGINES}
        self.emitted = {}
        self.scopes = {e: [] for e in self.ENGINES}
        self.cur_scope = "init"

    def set_scope(self, name):
        self.cur_scope = name

    def wait(self, eng, sem, val):
        if val is None or val <= 0:
            return
        assert val <= self.emitted.get(sem, 0), \
            f"plan deadlock: {eng} waits {sem}>={val} emitted={self.emitted.get(sem, 0)}"
        self.ops[eng].append(lambda e, s=sem, v=val: e.wait_ge(SEMS[s], v))
        self.scopes[eng].append(self.cur_scope)

    def op(self, eng, fn, inc=None, inc_n=1):
        self.scopes[eng].append(self.cur_scope)
        if inc is None:
            self.ops[eng].append(fn)
            return None
        self.emitted[inc] = self.emitted.get(inc, 0) + inc_n
        val = self.emitted[inc]
        self.ops[eng].append(
            lambda e, f=fn, s=inc, n=inc_n: f(e).then_inc(SEMS[s], n))
        return val

    def dma(self, eng, out, in_, q):
        return self.op(eng, lambda e, o=out, i=in_: e.dma_start(out=o, in_=i),
                       inc=q, inc_n=16)


class Arena:
    def __init__(self, tensor, nbytes, esize):
        self.t, self.nbytes, self.esize = tensor, nbytes, esize
        self.off = 0
        self.marks = []
        self.peak = 0

    def alloc(self, parts, elems):
        off = (self.off + self.esize * 16 - 1) // (self.esize * 16) * (self.esize * 16)
        self.off = off + elems * self.esize
        self.peak = max(self.peak, self.off)
        assert self.off <= self.nbytes, f"arena overflow {self.off}>{self.nbytes}"
        return self.t[0:parts, off // self.esize:off // self.esize + elems]

    def mark(self):
        self.marks.append(self.off)

    def release(self):
        self.off = self.marks.pop()


def build(cfg: Cfg, debug=False, DBG_TAPS=False, DBG_T=0):
    C = cfg
    NST, RDT, KC, EDBL = C.NST, C.RDT, C.KC, C.EDBL
    nc = bass.Bass("TRN2", target_bir_lowering=False)

    def ein(name, shape, dt):
        return nc.dram_tensor(name, list(shape), dt, kind="ExternalInput").ap()

    hT = ein("hT", (C.DM, C.L), F32)
    winT = ein("winT", (C.DM, 2 * C.DIN), F32)
    wxT = ein("wxT", (C.DIN, EDBL), F32)
    wdtT = ein("wdtT", (RDT, C.DIN), F32)
    wosl = ein("wosl", (C.DIN, C.DMS), F32)
    convw = ein("convw", (128, C.NJ * KC), F32)
    convb = ein("convb", (128, C.NJ), F32)
    dtb = ein("dtb", (128, C.NJ), F32)
    alog = ein("alog", (128, C.NJ * NST), F32)
    dvec = ein("dvec", (128, C.NJ), F32)
    lutv = ein("lutv", (1, 16), F32)
    gidx = ein("gidx", (C.DM, C.OTP * 128), I32)
    uidx = ein("uidx", (C.DM, C.OTP * 128), I32)
    didx = ein("didx", (C.INTER, C.DMS), I32)
    Ht = ein("Ht", (C.NJ, C.NJ, 128, 128), BF16)
    I128d = ein("I128", (128, 128), BF16)
    ones128 = ein("ones128", (128, 1), F32)

    outT = nc.dram_tensor("outT", [C.DM, C.L], F32, kind="ExternalOutput").ap()

    xnT_d = nc.dram_tensor("xnT_d", [C.DM, C.L], BF16).ap()
    u_d = nc.dram_tensor("u_d", [C.DIN, C.L], BF16).ap()
    sz_d = nc.dram_tensor("sz_d", [C.DIN, C.L], BF16).ap()
    bc_d = nc.dram_tensor("bc_d", [NST, 2 * C.L], BF16).ap()
    ln_d = nc.dram_tensor("ln_d", [2, C.L], F32).ap()
    y_d = nc.dram_tensor("y_d", [C.DIN, C.L], BF16).ap()
    gT_d = nc.dram_tensor("gT_d", [C.DM, C.L], BF16).ap()
    nmt_l = C.DMS // 128
    wg_sl = nc.dram_tensor("wg_sl", [C.OTP * C.NKD, 128, 128], BF16)
    wu_sl = nc.dram_tensor("wu_sl", [C.OTP * C.NKD, 128, 128], BF16)
    wd_sl = nc.dram_tensor("wd_sl", [nmt_l * C.NOT, 128, 128], BF16)
    w2_sl = nc.dram_tensor("w2_sl", [C.NJ, 128, C.DMS], BF16)
    wg_d = nc.dram_tensor("wg_d", [C.NC * C.OTP * C.NKD, 128, 128], BF16).ap()
    wu_d = nc.dram_tensor("wu_d", [C.NC * C.OTP * C.NKD, 128, 128], BF16).ap()
    wd_d = nc.dram_tensor("wd_d", [C.NC * nmt_l * C.NOT, 128, 128], BF16).ap()
    w2_d = nc.dram_tensor("w2_d", [C.NC * C.NJ, 128, C.DMS], BF16).ap()
    scd_d = nc.dram_tensor("scd_d", [2, 128, C.CL], F32).ap()
    car_d = nc.dram_tensor("car_d", [128, C.NJ * C.NST], F32).ap()
    scdB_d = nc.dram_tensor("scdB_d", [3, 128, C.CL], BF16).ap()

    pl = Plan()
    es = ExitStack()

    AFB = 44 * 1024
    ABB = 143 * 1024
    arenaF_t = es.enter_context(nc.sbuf_tensor("arenaF", [128, AFB // 4], F32))
    arenaB_t = es.enter_context(nc.sbuf_tensor("arenaB", [128, ABB // 2], BF16))
    arF = Arena(arenaF_t, AFB, 4)
    arB = Arena(arenaB_t, ABB, 2)

    def g_sb(name, shape, dt):
        return es.enter_context(nc.sbuf_tensor(name, list(shape), dt))

    i128 = g_sb("i128s", (128, 128), BF16)
    ones_s = g_sb("ones_s", (128, 1), F32)
    lutbc = g_sb("lutbc", (128, 16), F32)
    convw_s = g_sb("convw_s", (128, C.NJ * KC), F32)
    convb_s = g_sb("convb_s", (128, C.NJ), F32)
    dtb_s = g_sb("dtb_s", (128, C.NJ), F32)
    dvec_s = g_sb("dvec_s", (128, C.NJ), F32)
    negA = g_sb("negA", (128, C.NJ * NST), F32)
    wdt_bf = g_sb("wdt_bf", (RDT, C.DIN), BF16)
    dtr_bf = g_sb("dtr_bf", (RDT, C.L), BF16)
    carry = g_sb("carry", (128, C.NJ * NST), F32)
    diagD = g_sb("diagD", (128, 128), BF16)
    idx_i = g_sb("idx_i", (128, max(C.OTP * 128, C.DMS)), I32)

    pq = [es.enter_context(nc.psum_tensor(f"pq{i}", [128, 1024], F32))
          for i in range(4)]

    qs, qa, qb, qc2, eT, eV, eS, cc = "qs", "qa", "qb", "qc2", "eT", "eV", "eS", "cc"
    qy = "qy"

    pl.set_scope("ph0")
    pl.dma("sync", i128[:], I128d[:], qs)
    pl.dma("sync", ones_s[:], ones128[:], qs)
    pl.dma("sync", lutbc[:], lutv[0:1, :].partition_broadcast(128), qs)
    pl.dma("sync", convw_s[:], convw[:], qs)
    pl.dma("sync", convb_s[:], convb[:], qs)
    pl.dma("sync", dtb_s[:], dtb[:], qs)
    pl.dma("sync", dvec_s[:], dvec[:], qs)
    v_consts = pl.dma("sync", negA[:], alog[:], qs)

    arF.mark()
    tmpA = arF.alloc(128, C.NJ * NST)
    wdt_f = arF.alloc(RDT, C.DIN)
    pl.wait("scalar", qs, v_consts)
    v_t = pl.op("scalar", lambda e: e.activation(out=tmpA, in_=negA[:], func=AF.Exp),
                inc=eS)
    pl.wait("vector", eS, v_t)
    v_negA = pl.op("vector", lambda e: e.tensor_scalar_mul(
        out=negA[:], in0=tmpA, scalar1=-1.0), inc=eV)
    v1 = pl.dma("sync", wdt_f, wdtT[:], qs)
    pl.wait("scalar", qs, v1)
    v_wdtc = pl.op("scalar", lambda e: e.copy(out=wdt_bf[:], in_=wdt_f), inc=eS)
    arF.release()

    arB.mark()
    matw = max(C.OTP * 128, C.DMS)
    idx_b = [arB.alloc(128, matw) for _ in range(2)]
    acc_bb = [arB.alloc(128, matw) for _ in range(2)]
    tmp_b = arB.alloc(128, matw)
    st = {"acc_store": {0: 0, 1: 0}, "idx_used": {0: 0, 1: 0}, "kt0": 0}

    def materialize(idx_dram, nrt, width, store_fn):
        for kt in range(nrt):
            gp = (st["kt0"] + kt) % 2
            pl.wait("sync", eV, max(st["idx_used"].values()))
            vq = pl.dma("sync", idx_i[0:128, 0:width],
                        idx_dram[kt * 128:(kt + 1) * 128, :], qa)
            pl.wait("vector", qa, vq)
            pl.wait("vector", qy, st["acc_store"][gp])
            pl.wait("vector", qs, v_consts)
            acc_b = acc_bb[gp]
            st["idx_used"][gp] = pl.op("vector", lambda e, w=width, gp=gp: e.tensor_copy(
                out=idx_b[gp][:, 0:w], in_=idx_i[0:128, 0:w]),
                inc=eV)
            pl.op("vector", lambda e, w=width, gp=gp, acc_b=acc_b: e.tensor_scalar(
                out=acc_b[:, 0:w], in0=idx_b[gp][:, 0:w], scalar1=0.0,
                scalar2=lutbc[:, 0:1], op0=OP.is_equal, op1=OP.mult))
            v = None
            for k in range(1, 16):
                pl.op("vector", lambda e, k=k, w=width, gp=gp: e.tensor_scalar(
                    out=tmp_b[:, 0:w], in0=idx_b[gp][:, 0:w], scalar1=float(k),
                    scalar2=lutbc[:, k:k + 1], op0=OP.is_equal, op1=OP.mult))
                v = pl.op("vector", lambda e, w=width, acc_b=acc_b: e.tensor_add(
                    out=acc_b[:, 0:w], in0=acc_b[:, 0:w], in1=tmp_b[:, 0:w]),
                    inc=eV if k == 15 else None)
            pl.wait("scalar", eV, v)
            last = 0
            for o_ap, i_ap in store_fn(kt, acc_b):
                last = pl.dma("scalar", o_ap, i_ap, qy)
            st["acc_store"][gp] = last
        st["kt0"] += nrt

    def gu_store(sl):
        def f(kt, acc_b):
            view = sl.ap().rearrange("(o k) p e -> o k p e", k=C.NKD)
            return [(view[o, kt, :, :], acc_b[:, o * 128:(o + 1) * 128])
                    for o in range(C.OTP)]
        return f

    def dn_store(kt, acc_b):
        view = wd_sl.ap().rearrange("(m k) p e -> m k p e", k=C.NOT)
        return [(view[m, kt, :, :], acc_b[:, m * 128:(m + 1) * 128])
                for m in range(nmt_l)]

    pl.set_scope("mat")
    materialize(gidx, C.NKD, C.OTP * 128, gu_store(wg_sl))
    materialize(uidx, C.NKD, C.OTP * 128, gu_store(wu_sl))
    materialize(didx, C.NOT, C.DMS, dn_store)
    v_mat_stores = pl.emitted.get(qy, 0)
    arB.release()

    pl.set_scope("w2")
    arB.mark()
    arF.mark()
    wo_b = arB.alloc(128, C.NJ * C.DMS)
    w2ev2 = [arB.alloc(128, C.DMS) for _ in range(2)]
    hsl = [arB.alloc(128, C.NJ * 128) for _ in range(2)]
    wo_f = arF.alloc(128, C.NJ * C.DMS)
    pl.wait("sync", eS, v_wdtc)
    pl.wait("sync", eV, v_negA)
    vq = pl.dma("sync", wo_f.rearrange("p (k e) -> p k e", e=C.DMS),
                wosl.rearrange("(k p) e -> p k e", p=128)[:], qs)
    pl.wait("vector", qs, vq)
    pl.wait("vector", qy, v_mat_stores)
    v_wocvt = pl.op("vector", lambda e: e.tensor_copy(out=wo_b, in_=wo_f), inc=eV)
    NSL = (C.DMS + 511) // 512
    v_w2ev = 0
    vW2mm = {}
    v_w2st2 = {}
    for m in range(C.NJ):
        buf = hsl[m % 2]
        if m >= 2:
            pl.wait("sync", eT, vW2mm[m - 2])
        vq = pl.dma("sync", buf.rearrange("p (k e) -> p k e", e=128),
                    Ht[m, :, :, :].rearrange("k p e -> p k e"), qs)
        pl.wait("tensor", qs, vq)
        pl.wait("tensor", eV, v_wocvt)
        pl.wait("tensor", eV, v_w2ev)
        vmm = None
        for kt in range(C.NJ):
            for si in range(NSL):
                s_, n_ = si * 512, min(512, C.DMS - si * 512)
                pw = pq[2][:, 0:512] if si == 0 else pq[2][:, 512:1024]
                vmm = pl.op("tensor", lambda e, b=buf, kt=kt, s_=s_, n_=n_, pw=pw: e.matmul(
                    out=pw[:, 0:n_], lhsT=b[:, kt * 128:(kt + 1) * 128],
                    rhs=wo_b[:, kt * C.DMS + s_:kt * C.DMS + s_ + n_],
                    start=(kt == 0), stop=(kt == C.NJ - 1)),
                    inc=eT if (kt == C.NJ - 1 and si == NSL - 1) else None)
        vW2mm[m] = vmm
        pl.wait("vector", eT, vmm)
        pl.wait("vector", qy, v_w2st2.get(m - 2, 0))
        for si in range(NSL):
            s_, n_ = si * 512, min(512, C.DMS - si * 512)
            pw = pq[2][:, 0:512] if si == 0 else pq[2][:, 512:1024]
            v_w2ev = pl.op("vector", lambda e, m=m, s_=s_, n_=n_, pw=pw: e.tensor_copy(
                out=w2ev2[m % 2][:, s_:s_ + n_], in_=pw[:, 0:n_]),
                inc=eV if si == NSL - 1 else None)
        pl.wait("scalar", eV, v_w2ev)
        v_w2st2[m] = pl.dma("scalar", w2_sl.ap()[m, :, :], w2ev2[m % 2], qy)
    v_w2_stores = pl.emitted.get(qy, 0)
    arB.release()
    arF.release()

    pl.wait("gpsimd", qy, v_mat_stores)
    for sl, full in ((wg_sl, wg_d), (wu_sl, wu_d), (wd_sl, wd_d)):
        pl.op("gpsimd", lambda e, s=sl, f=full: e.collective_compute(
            "AllGather", OP.bypass, [list(range(C.NC))],
            [s.ap().opt()], [f.opt()]), inc=cc)
    pl.wait("gpsimd", qy, v_w2_stores)
    v_cc4 = pl.op("gpsimd", lambda e: e.collective_compute(
        "AllGather", OP.bypass, [list(range(C.NC))],
        [w2_sl.ap().opt()], [w2_d.opt()]), inc=cc)

    pl.set_scope("ln")
    arB.mark()
    xnT = arB.alloc(128, C.NKD * C.L)
    arF.mark()
    CBN = C.CBLN
    NCBN = C.L // CBN
    hTc = arF.alloc(128, C.NKD * CBN)
    sq2 = [arF.alloc(128, CBN) for _ in range(2)]
    r_mean = arF.alloc(1, CBN)
    r_t = arF.alloc(1, CBN)
    r_var = arF.alloc(1, CBN)
    r_rstd = arF.alloc(1, CBN)
    r_ms = arF.alloc(1, CBN)
    bc_rstd = arF.alloc(128, CBN)
    bc_ms = arF.alloc(128, CBN)
    pln0 = pq[3][0:1, 0:512]
    pln1 = pq[3][0:1, 512:1024]
    v_apply, v_rows = 0, 0
    vstat_hist = {}
    for cb in range(NCBN):
        lo = cb * CBN
        if cb > 0:
            pl.wait("sync", eV, v_apply)
        else:
            pl.wait("sync", eV, v_negA)
            pl.wait("sync", eV, v_wocvt)
            pl.wait("sync", eS, v_wdtc)
        vq = pl.dma("sync", hTc.rearrange("p (k l) -> p k l", l=CBN),
                    hT.rearrange("(k p) l -> p k l", p=128)[:, :, lo:lo + CBN], qs)
        pl.wait("tensor", qs, vq)
        pl.wait("tensor", eV, v_rows)
        for kt in range(C.NKD):
            pl.op("tensor", lambda e, kt=kt: e.matmul(
                out=pln0[:, 0:CBN], lhsT=ones_s[:],
                rhs=hTc[:, kt * CBN:(kt + 1) * CBN],
                start=(kt == 0), stop=(kt == C.NKD - 1)))
        pl.wait("scalar", qs, vq)
        vstat = None
        for kt in range(C.NKD):
            pl.wait("scalar", eT, vstat_hist.get((cb, kt - 2), 0))
            vsq = pl.op("scalar", lambda e, kt=kt: e.activation(
                out=sq2[kt % 2], in_=hTc[:, kt * CBN:(kt + 1) * CBN],
                func=AF.Square), inc=eS)
            pl.wait("tensor", eS, vsq)
            vstat = pl.op("tensor", lambda e, kt=kt: e.matmul(
                out=pln1[:, 0:CBN], lhsT=ones_s[:], rhs=sq2[kt % 2],
                start=(kt == 0), stop=(kt == C.NKD - 1)), inc=eT)
            vstat_hist[(cb, kt)] = vstat
        pl.wait("vector", eT, vstat)
        pl.op("vector", lambda e: e.tensor_scalar_mul(
            out=r_mean, in0=pln0[:, 0:CBN], scalar1=1.0 / C.DM))
        pl.op("vector", lambda e: e.tensor_scalar_mul(
            out=r_t, in0=pln1[:, 0:CBN], scalar1=1.0 / C.DM))
        pl.op("vector", lambda e: e.tensor_mul(out=r_var, in0=r_mean, in1=r_mean))
        pl.op("vector", lambda e: e.tensor_sub(out=r_var, in0=r_t, in1=r_var))
        v_rows = pl.op("vector", lambda e: e.tensor_scalar_add(
            out=r_var, in0=r_var, scalar1=C.EPS), inc=eV)
        pl.wait("scalar", eV, v_rows)
        v_ln = pl.op("scalar", lambda e: e.activation(out=r_rstd, in_=r_var,
                                                      func=AF.Ln), inc=eS)
        pl.wait("vector", eS, v_ln)
        v_half = pl.op("vector", lambda e: e.tensor_scalar_mul(
            out=r_rstd, in0=r_rstd, scalar1=-0.5), inc=eV)
        pl.wait("scalar", eV, v_half)
        v_rst = pl.op("scalar", lambda e: e.activation(out=r_rstd, in_=r_rstd,
                                                       func=AF.Exp), inc=eS)
        pl.wait("vector", eS, v_rst)
        v_msr = pl.op("vector", lambda e: e.tensor_mul(out=r_ms, in0=r_mean,
                                                       in1=r_rstd), inc=eV)
        pl.wait("sync", eV, v_msr)
        pl.dma("sync", ln_d[0:1, lo:lo + CBN], r_rstd, qs)
        v_rst_st = pl.dma("sync", ln_d[1:2, lo:lo + CBN], r_ms, qs)
        pl.wait("sync", qs, v_rst_st)
        pl.dma("sync", bc_rstd, ln_d[0:1, lo:lo + CBN].partition_broadcast(128), qs)
        vbc = pl.dma("sync", bc_ms, ln_d[1:2, lo:lo + CBN].partition_broadcast(128), qs)
        pl.wait("vector", qs, vbc)
        if cb == 0:
            pl.wait("vector", qy, v_w2_stores)
        for kt in range(C.NKD):
            pl.op("vector", lambda e, kt=kt: e.tensor_mul(
                out=hTc[:, kt * CBN:(kt + 1) * CBN],
                in0=hTc[:, kt * CBN:(kt + 1) * CBN], in1=bc_rstd))
            v_apply = pl.op("vector", lambda e, kt=kt, lo=lo: e.tensor_sub(
                out=xnT[:, kt * C.L + lo:kt * C.L + lo + CBN],
                in0=hTc[:, kt * CBN:(kt + 1) * CBN], in1=bc_ms),
                inc=eV if kt == C.NKD - 1 else None)
        pl.wait("scalar", eV, v_apply)
        pl.dma("scalar",
               xnT_d.rearrange("(k p) l -> p k l", p=128)[:, :, lo:lo + CBN],
               xnT.rearrange("p (k l) -> p k l", l=C.L)[:, :, lo:lo + CBN], qy)
    v_ln_done = v_apply
    v_xn_spill = pl.emitted.get(qy, 0)
    arF.release()

    pl.set_scope("m1")
    arF.mark()
    arB.mark()
    wsl_f = [arF.alloc(128, C.NKD * 128) for _ in range(4)]
    wsl_b = [arB.alloc(128, C.NKD * 128) for _ in range(4)]
    x_bf = [arB.alloc(128, C.L) for _ in range(2)]
    u_bf1 = [arB.alloc(128, C.L) for _ in range(2)]
    sz_bf1 = [arB.alloc(128, C.L) for _ in range(2)]
    diag4b = [arB.alloc(128, KC * 128) for _ in range(2)]
    LH = min(1024, C.L)
    NH = C.L // LH
    SL2 = min(512, LH)
    NSL2 = LH // SL2
    winT_r = winT.rearrange("(k p) e -> p k e", p=128)
    m1_marks = {}

    for j in range(C.NJ):
        par = j % 2
        px, pz = (pq[0], pq[1]) if par == 0 else (pq[2], pq[3])
        mk = m1_marks.get(j - 2, {})
        pl.wait("sync", eS, mk.get("wcvt"))
        if j == 0:
            pl.wait("sync", eV, v_ln_done)
        pl.dma("sync", wsl_f[2 * par], winT_r[:, :, j * 128:(j + 1) * 128], qs)
        v2 = pl.dma("sync", wsl_f[2 * par + 1],
                    winT_r[:, :, C.DIN + j * 128:C.DIN + (j + 1) * 128], qs)
        pl.wait("scalar", qs, v2)
        pl.wait("scalar", eT, mk.get("conv"))
        pl.op("scalar", lambda e, p=par: e.copy(out=wsl_b[2 * p], in_=wsl_f[2 * p]))
        v_wcvt = pl.op("scalar", lambda e, p=par: e.copy(
            out=wsl_b[2 * p + 1], in_=wsl_f[2 * p + 1]), inc=eS)
        pl.wait("scalar", eT, mk.get("conv"))
        pl.wait("scalar", qy, mk.get("spill", 0))
        pl.wait("tensor", eS, v_wcvt)
        pl.wait("tensor", eV, v_ln_done)
        v_xev = None
        for h2 in range(NH):
            vx = None
            for kt in range(C.NKD):
                for si in range(NSL2):
                    vx = pl.op("tensor",
                        lambda e, p=par, kt=kt, h2=h2, s_=si * SL2, px=px: e.matmul(
                            out=px[:, s_:s_ + SL2],
                            lhsT=wsl_b[2 * p][:, kt * 128:(kt + 1) * 128],
                            rhs=xnT[:, kt * C.L + h2 * LH + s_:
                                    kt * C.L + h2 * LH + s_ + SL2],
                            start=(kt == 0), stop=(kt == C.NKD - 1)),
                        inc=eT if (kt == C.NKD - 1 and si == NSL2 - 1) else None)
            pl.wait("scalar", eT, vx)
            v_xev = pl.op("scalar", lambda e, p=par, h2=h2, px=px: e.copy(
                out=x_bf[p][:, h2 * LH:(h2 + 1) * LH], in_=px[:, 0:LH]), inc=eS)
            if h2 + 1 < NH:
                pl.wait("tensor", eS, v_xev)
        v_szev = None
        for h2 in range(NH):
            vz = None
            for kt in range(C.NKD):
                for si in range(NSL2):
                    vz = pl.op("tensor",
                        lambda e, p=par, kt=kt, h2=h2, s_=si * SL2, pz=pz: e.matmul(
                            out=pz[:, s_:s_ + SL2],
                            lhsT=wsl_b[2 * p + 1][:, kt * 128:(kt + 1) * 128],
                            rhs=xnT[:, kt * C.L + h2 * LH + s_:
                                    kt * C.L + h2 * LH + s_ + SL2],
                            start=(kt == 0), stop=(kt == C.NKD - 1)),
                        inc=eT if (kt == C.NKD - 1 and si == NSL2 - 1) else None)
            pl.wait("scalar", eT, vz)
            v_szev = pl.op("scalar", lambda e, p=par, h2=h2, pz=pz: e.activation(
                out=sz_bf1[p][:, h2 * LH:(h2 + 1) * LH], in_=pz[:, 0:LH],
                func=AF.Silu), inc=eS)
            if h2 + 1 < NH:
                pl.wait("tensor", eS, v_szev)
        pl.wait("vector", qs, v_consts)
        if j >= 2:
            pl.wait("vector", eT, m1_marks[j - 2]["conv"])
        diag4 = diag4b[par]
        vd = None
        for k in range(KC):
            vd = pl.op("vector", lambda e, k=k, j=j, diag4=diag4: e.tensor_scalar(
                out=diag4[:, k * 128:(k + 1) * 128], in0=i128[:],
                scalar1=convw_s[:, j * KC + k:j * KC + k + 1], scalar2=None,
                op0=OP.mult), inc=eV if k == KC - 1 else None)
        pl.wait("tensor", eS, v_xev)
        pl.wait("tensor", eV, vd)
        v_uev = None
        vc = None
        CVS = min(512, LH)
        for h2 in range(NH):
            for q in range(LH // CVS):
                for k in range(KC):
                    sft = (KC - 1) - k
                    a = h2 * LH + q * CVS
                    o_lo = max(a, sft)
                    n_ = a + CVS - o_lo
                    vc = pl.op("tensor",
                        lambda e, p=par, k=k, o_lo=o_lo, n_=n_, sft=sft, px=px, h2=h2, diag4=diag4: e.matmul(
                            out=px[:, o_lo - h2 * LH:o_lo - h2 * LH + n_],
                            lhsT=diag4[:, k * 128:(k + 1) * 128],
                            rhs=x_bf[p][:, o_lo - sft:o_lo - sft + n_],
                            start=(k == 0), stop=(k == KC - 1)),
                        inc=eT if (k == KC - 1 and q == LH // CVS - 1) else None)
            pl.wait("scalar", eT, vc)
            v_uev = pl.op("scalar", lambda e, p=par, j=j, h2=h2, px=px: e.activation(
                out=u_bf1[p][:, h2 * LH:(h2 + 1) * LH], in_=px[:, 0:LH],
                func=AF.Silu, bias=convb_s[:, j:j + 1]), inc=eS)
            if h2 + 1 < NH:
                pl.wait("tensor", eS, v_uev)
        pl.wait("scalar", eS, v_uev)
        pl.dma("scalar", u_d[j * 128:(j + 1) * 128, :], u_bf1[par], qy)
        pl.wait("scalar", eS, v_szev)
        v_sp = pl.dma("scalar", sz_d[j * 128:(j + 1) * 128, :], sz_bf1[par], qy)
        m1_marks[j] = {"wcvt": v_wcvt, "conv": vc, "uev": v_uev, "spill": v_sp}
    v_m1_S = m1_marks[C.NJ - 1]["uev"]
    v_m1_sp = m1_marks[C.NJ - 1]["spill"]

    pl.set_scope("m1b")
    arB.release()
    arF.release()
    arB.mark()
    arF.mark()
    wx_f = arF.alloc(128, C.NJ * EDBL)
    wx_bf = arB.alloc(128, C.NJ * EDBL)
    ubk = [arB.alloc(128, C.L) for _ in range(2)]
    bcev = arB.alloc(2 * NST, C.L)
    pl.wait("sync", eS, m1_marks[C.NJ - 1]["wcvt"])
    vq = pl.dma("sync", wx_f.rearrange("p (j e) -> p j e", e=EDBL),
                wxT.rearrange("(j p) e -> p j e", p=128)[:], qs)
    pl.wait("scalar", qs, vq)
    pl.wait("scalar", eT, m1_marks[C.NJ - 1]["conv"])
    v_wxc = pl.op("scalar", lambda e: e.copy(out=wx_bf, in_=wx_f), inc=eS)
    NQ = C.L // min(512, C.L)
    SQ = C.L // NQ
    vmmJ = {}
    pl.wait("sync", qy, v_m1_sp)
    for j in range(C.NJ):
        if j >= 2:
            pl.wait("sync", eT, vmmJ[j - 2])
        v_uld = pl.dma("sync", ubk[j % 2], u_d[j * 128:(j + 1) * 128, :], qa)
        pl.wait("tensor", qa, v_uld)
        if j == 0:
            pl.wait("tensor", eS, v_wxc)
            pl.wait("tensor", eS, v_m1_S)
            pl.wait("tensor", eS, v_wdtc)
        vlast = None
        for qi in range(NQ):
            s_ = qi * SQ
            pdt = pq[0] if s_ < 1024 else pq[1]
            pl.op("tensor", lambda e, j=j, s_=s_, pdt=pdt: e.matmul(
                out=pdt[0:RDT, s_ % 1024:s_ % 1024 + SQ],
                lhsT=wx_bf[:, j * EDBL:j * EDBL + RDT],
                rhs=ubk[j % 2][:, s_:s_ + SQ],
                start=(j == 0), stop=(j == C.NJ - 1)))
        for qi in range(NQ):
            s_ = qi * SQ
            pbc = pq[2] if s_ < 1024 else pq[3]
            vlast = pl.op("tensor", lambda e, j=j, s_=s_, pbc=pbc: e.matmul(
                out=pbc[0:2 * NST, s_ % 1024:s_ % 1024 + SQ],
                lhsT=wx_bf[:, j * EDBL + RDT:(j + 1) * EDBL],
                rhs=ubk[j % 2][:, s_:s_ + SQ],
                start=(j == 0), stop=(j == C.NJ - 1)), inc=eT)
        vmmJ[j] = vlast
    v_xdbl = vmmJ[C.NJ - 1]
    pl.wait("scalar", eT, v_xdbl)
    for qi in range(NQ):
        s_ = qi * SQ
        pdt = pq[0] if s_ < 1024 else pq[1]
        pl.op("scalar", lambda e, s_=s_, pdt=pdt: e.copy(
            out=dtr_bf[:, s_:s_ + SQ], in_=pdt[0:RDT, s_ % 1024:s_ % 1024 + SQ]))
    v_bcev = None
    for qi in range(NQ):
        s_ = qi * SQ
        pbc = pq[2] if s_ < 1024 else pq[3]
        v_bcev = pl.op("scalar", lambda e, s_=s_, pbc=pbc: e.copy(
            out=bcev[:, s_:s_ + SQ], in_=pbc[0:2 * NST, s_ % 1024:s_ % 1024 + SQ]),
            inc=eS if qi == NQ - 1 else None)
    pl.wait("scalar", eS, v_bcev)
    pl.dma("scalar", bc_d[:, 0:C.L], bcev[0:NST, :], qy)
    v_bcst = pl.dma("scalar", bc_d[:, C.L:2 * C.L], bcev[NST:2 * NST, :], qy)
    arB.release()
    arF.release()
    arB.release()

    arF.mark()
    arB.mark()
    delta_f = arF.alloc(128, C.CL)
    ch1 = arF.alloc(128, C.CL)
    ch2 = arF.alloc(128, C.CL)
    dA2 = [arF.alloc(128, C.CL) for _ in range(2)]
    out_sb = [arF.alloc(128, C.CBM) for _ in range(2)]
    hT_t = [arF.alloc(128, C.CBM) for _ in range(2)]
    du_b = arB.alloc(128, C.CL)
    dBu_b = arB.alloc(128, C.CL)
    h_b = arB.alloc(128, C.CL)
    ytmp = [arB.alloc(128, C.CL) for _ in range(4)]
    bcr = [arB.alloc(128, 2 * C.CL) for _ in range(3)]
    u_sc = [arB.alloc(128, C.CL) for _ in range(2)]
    sz_sc = [arB.alloc(128, C.CL) for _ in range(2)]
    ybf = [arB.alloc(128, C.CL) for _ in range(2)]
    xn_fc = arB.alloc(128, C.NKD * C.CB)
    h_ffn = arB.alloc(128, C.NOT * C.CB)
    sg_b = arB.alloc(128, C.CB)
    th_f = arF.alloc(128, C.CB)
    th_a = arF.alloc(128, C.CB)
    gwsl = [arB.alloc(128, C.NKD * 128) for _ in range(2)]
    uwsl = [arB.alloc(128, C.NKD * 128) for _ in range(2)]
    NOTH = (C.NOT + 3) // 4
    dwsl = arB.alloc(128, NOTH * 128)
    w2sl_b = arB.alloc(128, C.NJ * 128)
    ysl = arB.alloc(128, C.NJ * C.CBM)
    gt_t = [arB.alloc(128, C.CBM) for _ in range(2)]

    pY = pq[0]
    pDl = pq[1][:, 0:512]
    pFg2 = [pq[2][:, 0:512], pq[2][:, 512:1024]]
    pFu2 = [pq[1][:, 512:1024], pq[3][:, 512:1024]]
    pM = pq[3][:, 0:512]

    CBL = C.CL // C.CB

    fstate = {
        "gt_store": {}, "yspill": {}, "yslot": {},
        "vhm_last": 0, "vsg_last": 0, "vdownev": 0, "vdown_last": 0,
        "vcomb": 0, "vout_st": {}, "vgt_read": 0,
        "vxn_free": 0, "vslab": {}, "vdsl": 0, "vw2sl": 0, "vysl": 0,
    }

    def gen_ffn(fc):
        pl.set_scope("ffn")
        lo = fc * C.CB
        pl.wait("sync", qy, v_xn_spill)
        pl.wait("sync", eT, fstate["vxn_free"])
        vxn = pl.dma(
            "sync", xn_fc.rearrange("p (k l) -> p k l", l=C.CB),
            xnT_d.rearrange("(k p) l -> p k l", p=128)[:, :, lo:lo + C.CB],
            qa)
        yield
        vg = {}
        vu = {}
        vth = {}

        def emit_mms(g):
            gp = g % 2
            c_, ot_l = divmod(g, C.OTP)
            base = (c_ * C.OTP + ot_l) * C.NKD
            pl.wait("sync", cc, 3)
            if g >= 2:
                pl.wait("sync", eT, vu[g - 2])
            pl.dma("sync", gwsl[gp].rearrange("p (t e) -> p t e", e=128),
                   wg_d.rearrange("t p e -> p t e")[:, base:base + C.NKD, :],
                   qa)
            return pl.dma("sync", uwsl[gp].rearrange("p (t e) -> p t e", e=128),
                          wu_d.rearrange("t p e -> p t e")[:, base:base + C.NKD, :],
                          qa)

        def emit_gu(g, vsl):
            gp = g % 2
            pl.wait("tensor", qa, vsl)
            pl.wait("tensor", qa, vxn)
            pl.wait("tensor", eV, fstate["vsg_last"] if g < 2 else vhm_g.get(g - 2, 0))
            if g == 0 and fc == 0:
                pl.wait("tensor", eS, v_bcev)
            for kt in range(C.NKD):
                vg[g] = pl.op("tensor", lambda e, kt=kt, gp=gp: e.matmul(
                    out=pFg2[gp][:, 0:C.CB], lhsT=gwsl[gp][:, kt * 128:(kt + 1) * 128],
                    rhs=xn_fc[:, kt * C.CB:(kt + 1) * C.CB],
                    start=(kt == 0), stop=(kt == C.NKD - 1)),
                    inc=eT if kt == C.NKD - 1 else None)
            for kt in range(C.NKD):
                vu[g] = pl.op("tensor", lambda e, kt=kt, gp=gp: e.matmul(
                    out=pFu2[gp][:, 0:C.CB], lhsT=uwsl[gp][:, kt * 128:(kt + 1) * 128],
                    rhs=xn_fc[:, kt * C.CB:(kt + 1) * C.CB],
                    start=(kt == 0), stop=(kt == C.NKD - 1)),
                    inc=eT if kt == C.NKD - 1 else None)

        def emit_act(g):
            gp = g % 2
            pl.wait("scalar", eT, vg[g])
            pl.wait("scalar", eV, fstate["vsg_last"])
            vth[g] = pl.op("scalar", lambda e, gp=gp: e.activation(
                out=th_f, in_=pFg2[gp][:, 0:C.CB], func=AF.Tanh, scale=0.5), inc=eS)

        vhm_g = {}

        def emit_hm(g):
            gp = g % 2
            pl.wait("vector", eS, vth[g])
            pl.wait("vector", eV, fstate["vhm_last"])
            if g == 0:
                pl.wait("vector", qy, fstate["gt_store"].get((fc - 1, C.NMT - 1), 0))
            pl.op("vector", lambda e: e.tensor_scalar(
                out=th_a, in0=th_f, scalar1=1.0, scalar2=0.5,
                op0=OP.add, op1=OP.mult))
            vsg = pl.op("vector", lambda e, gp=gp: e.tensor_mul(
                out=sg_b, in0=th_a, in1=pFg2[gp][:, 0:C.CB]), inc=eV)
            fstate["vsg_last"] = vsg
            pl.wait("vector", eT, vu[g])
            if g == 0:
                pl.wait("vector", eT, fstate["vdown_last"])
            vhm = pl.op("vector", lambda e, g=g, gp=gp: e.tensor_mul(
                out=h_ffn[:, g * C.CB:(g + 1) * C.CB], in0=sg_b,
                in1=pFu2[gp][:, 0:C.CB]), inc=eV)
            vhm_g[g] = vhm
            fstate["vhm_last"] = vhm

        vsl_next = emit_mms(0)
        emit_gu(0, vsl_next)
        yield
        for g in range(1, C.NOT):
            vsl_next = emit_mms(g)
            yield
            emit_gu(g, vsl_next)
            yield
            emit_act(g - 1)
            emit_hm(g - 1)
            yield
        emit_act(C.NOT - 1)
        emit_hm(C.NOT - 1)
        yield
        fstate["vxn_free"] = vu[C.NOT - 1]
        vhm_all = fstate["vhm_last"]
        for mt in range(C.NMT):
            c_, mt_l = divmod(mt, nmt_l)
            base = (c_ * nmt_l + mt_l) * C.NOT
            vdn = None
            for half in range(4):
                k0 = half * NOTH
                k1 = min(C.NOT, k0 + NOTH)
                if k1 <= k0:
                    continue
                pl.wait("sync", eT, fstate["vdsl"])
                vdsl = pl.dma("sync", dwsl.rearrange("p (t e) -> p t e", e=128)
                              [:, 0:k1 - k0, :],
                              wd_d.rearrange("t p e -> p t e")
                              [:, base + k0:base + k1, :], qa)
                yield
                pl.wait("tensor", qa, vdsl)
                pl.wait("tensor", eV, vhm_all)
                if half == 0:
                    pl.wait("tensor", eV, fstate["vdownev"])
                    pl.wait("tensor", eV, fstate["vsg_last"])
                for ko in range(k0, k1):
                    vdn = pl.op("tensor", lambda e, ko=ko, k0=k0: e.matmul(
                        out=pFg2[0][:, 0:C.CB],
                        lhsT=dwsl[:, (ko - k0) * 128:(ko - k0 + 1) * 128],
                        rhs=h_ffn[:, ko * C.CB:(ko + 1) * C.CB],
                        start=(ko == 0), stop=(ko == C.NOT - 1)),
                        inc=eT if ko == k1 - 1 else None)
                    if ko % 8 == 7:
                        yield
                fstate["vdsl"] = vdn
            if mt == C.NMT - 1:
                fstate["vdown_last"] = vdn
            pl.wait("scalar", eT, vdn)
            pl.wait("scalar", eV, fstate["vsg_last"])
            vth2 = pl.op("scalar", lambda e: e.activation(
                out=th_f, in_=pFg2[0][:, 0:C.CB], func=AF.Tanh, scale=0.5), inc=eS)
            pl.wait("vector", eS, vth2)
            pl.wait("vector", eV, fstate["vhm_last"])
            pl.wait("vector", qy, fstate["gt_store"].get((fc, mt - 1), 0))
            pl.op("vector", lambda e: e.tensor_scalar(
                out=th_a, in0=th_f, scalar1=1.0, scalar2=0.5,
                op0=OP.add, op1=OP.mult))
            vgt = pl.op("vector", lambda e: e.tensor_mul(
                out=sg_b, in0=th_a, in1=pFg2[0][:, 0:C.CB]), inc=eV)
            fstate["vdownev"] = vgt
            fstate["vsg_last"] = vgt
            pl.wait("scalar", eV, vgt)
            fstate["gt_store"][(fc, mt)] = pl.dma(
                "scalar", gT_d[mt * 128:(mt + 1) * 128, lo:lo + C.CB], sg_b, qy)
            yield

    def gen_m4(s):
        pl.set_scope("m4")
        lc = (s * C.CBM) // C.CL
        lo = s * C.CBM
        CBM = C.CBM
        pl.wait("sync", qy, fstate["yspill"][lc])
        pl.wait("sync", eT, fstate["vysl"])
        vy = pl.dma("sync", ysl.rearrange("p (k l) -> p k l", l=CBM),
                    y_d.rearrange("(k p) l -> p k l", p=128)[:, :, lo:lo + CBM],
                    qa)
        yield
        fcg = lo // C.CB
        for mt in range(C.NMT):
            mp = mt % 2
            c_ = (mt * 128) // C.DMS
            colo = (mt * 128) % C.DMS
            pl.wait("sync", cc, v_cc4)
            pl.wait("sync", eT, fstate["vw2sl"])
            vw2 = pl.dma("sync", w2sl_b.rearrange("p (k e) -> p k e", e=128),
                         w2_d.rearrange("(r k) p e -> r p k e", k=C.NJ)
                         [c_, :, :, colo:colo + 128], qa)
            pl.wait("sync", qy, fstate["gt_store"][(fcg, mt)])
            pl.wait("sync", eV, fstate["vgt_read"])
            pl.wait("sync", qy, fstate["vout_st"].get(mp, 0))
            pl.dma("sync", gt_t[mp], gT_d[mt * 128:(mt + 1) * 128, lo:lo + CBM], qs)
            vht = pl.dma("sync", hT_t[mp],
                         hT[mt * 128:(mt + 1) * 128, lo:lo + CBM], qs)
            yield
            pl.wait("tensor", qa, vy)
            pl.wait("tensor", qa, vw2)
            pl.wait("tensor", eV, fstate["vcomb"])
            vmm = None
            for kt in range(C.NJ):
                vmm = pl.op("tensor", lambda e, kt=kt: e.matmul(
                    out=pM[:, 0:CBM], lhsT=w2sl_b[:, kt * 128:(kt + 1) * 128],
                    rhs=ysl[:, kt * CBM:(kt + 1) * CBM],
                    start=(kt == 0), stop=(kt == C.NJ - 1)),
                    inc=eT if kt == C.NJ - 1 else None)
                if kt % 8 == 7:
                    yield
            fstate["vw2sl"] = vmm
            fstate["vysl"] = vmm
            pl.wait("vector", eT, vmm)
            pl.wait("vector", qs, vht)
            pl.wait("vector", qy, fstate["vout_st"].get(mp, 0))
            pl.op("vector", lambda e, mp=mp: e.tensor_mul(
                out=gt_t[mp], in0=gt_t[mp], in1=pM[:, 0:CBM]))
            vcb = pl.op("vector", lambda e, mp=mp: e.tensor_add(
                out=out_sb[mp], in0=gt_t[mp], in1=hT_t[mp]), inc=eV)
            fstate["vcomb"] = vcb
            fstate["vgt_read"] = vcb
            pl.wait("scalar", eV, vcb)
            fstate["vout_st"][mp] = pl.dma(
                "scalar", outT[mt * 128:(mt + 1) * 128, lo:lo + CBM],
                out_sb[mp], qy)
            yield

    genq = deque()
    for fc in range(C.NCB):
        genq.append(gen_ffn(fc))

    def pump(k=1):
        for _ in range(k):
            while genq:
                try:
                    next(genq[0])
                    return
                except StopIteration:
                    genq.popleft()
            return

    vstep = {}
    vyadd = {}
    vtD = {}
    vgate = {}
    vspU = {}
    v_bc = {}
    v_du = {}
    v_dAw = {}
    NTOT = C.NLC * C.NJ

    def emit_bcast(st_):
        if st_ >= NTOT * NST:
            return
        lc2, r = divmod(st_, C.NJ * NST)
        j2, n2 = divmod(r, NST)
        par = st_ % 3
        pl.wait("sync", qy, v_bcst)
        pl.wait("sync", eV, vstep.get(st_ - 3, 0))
        lo2 = lc2 * C.CL
        src = bc_d[n2:n2 + 1, :].rearrange("p (s l) -> p s l", s=2)[:, :, lo2:lo2 + C.CL]
        v_bc[st_] = pl.dma("sync",
                           bcr[par].rearrange("p (s l) -> p s l", s=2),
                           src.partition_broadcast(128), qb)

    def emit_uld(t2):
        if t2 >= NTOT:
            return
        lc2, j2 = divmod(t2, C.NJ)
        par = t2 % 2
        pl.wait("sync", qy, v_m1_sp)
        pl.wait("sync", eV, v_du.get(t2 - 2, 0))
        pl.wait("sync", eT, vtD.get(t2 - 2, 0))
        pl.wait("sync", eV, vgate.get(t2 - 2, 0))
        lo2 = lc2 * C.CL
        pl.dma("sync", u_sc[par], u_d[j2 * 128:(j2 + 1) * 128, lo2:lo2 + C.CL], qa)
        vspU[t2] = pl.dma("sync", sz_sc[par],
                          sz_d[j2 * 128:(j2 + 1) * 128, lo2:lo2 + C.CL], qa)

    for _st in range(3):
        emit_bcast(_st)
    emit_uld(0)
    emit_uld(1)

    for lc in range(C.NLC):
        lo = lc * C.CL
        for j in range(C.NJ):
            t = lc * C.NJ + j
            pl.set_scope(f"scan{lc}")
            par = t % 2
            NQ2 = C.CL // min(512, C.CL)
            SQ2 = C.CL // NQ2
            vsp = None
            for si in range(NQ2):
                s_ = si * SQ2
                pl.wait("tensor", eS, v_dAw.get((t, si - 1), v_dAw.get((t - 1, NQ2 - 1), 0)))
                if t == 0 and si == 0:
                    pl.wait("tensor", eS, v_bcev)
                vdt = pl.op("tensor", lambda e, j=j, lo=lo, s_=s_: e.matmul(
                    out=pDl[:, 0:SQ2],
                    lhsT=wdt_bf[:, j * 128:(j + 1) * 128],
                    rhs=dtr_bf[:, lo + s_:lo + s_ + SQ2],
                    start=True, stop=True), inc=eT)
                pl.wait("scalar", eT, vdt)
                if si == 0:
                    pl.wait("scalar", eV, v_du.get(t - 1, 0))
                pl.op("scalar", lambda e, j=j, s_=s_: e.activation(
                    out=delta_f[:, s_:s_ + SQ2], in_=pDl[:, 0:SQ2],
                    func=AF.Identity, bias=dtb_s[:, j:j + 1]))
                v_dAw[(t, si)] = pl.op("scalar", lambda e, s_=s_: e.activation(
                    out=ch1[:, s_:s_ + SQ2], in_=delta_f[:, s_:s_ + SQ2],
                    func=AF.Abs), inc=eS)
                pl.op("scalar", lambda e, s_=s_: e.activation(
                    out=ch2[:, s_:s_ + SQ2], in_=ch1[:, s_:s_ + SQ2],
                    func=AF.Exp, scale=-1.0))
                pl.op("scalar", lambda e, s_=s_: e.activation(
                    out=ch1[:, s_:s_ + SQ2], in_=ch2[:, s_:s_ + SQ2],
                    func=AF.Ln, bias=1.0))
                vsp = pl.op("scalar", lambda e, s_=s_: e.activation(
                    out=ch2[:, s_:s_ + SQ2], in_=delta_f[:, s_:s_ + SQ2],
                    func=AF.Relu), inc=eS)
            pl.wait("vector", eS, vsp)
            pl.op("vector", lambda e: e.tensor_add(out=delta_f, in0=ch1, in1=ch2))
            pl.wait("vector", qa, vspU[t])
            v_du[t] = pl.op("vector", lambda e, par=par: e.tensor_mul(
                out=du_b, in0=delta_f, in1=u_sc[par]), inc=eV)
            pl.wait("vector", eT, vtD.get(t - 1, 0))
            vdg = pl.op("vector", lambda e, j=j: e.tensor_scalar(
                out=diagD[:], in0=i128[:],
                scalar1=dvec_s[:, j:j + 1], scalar2=None, op0=OP.mult), inc=eV)
            for n in range(NST):
                st_ = t * NST + n
                sp = st_ % 2
                spy = st_ % 4
                pl.wait("scalar", eV, v_du[t])
                pl.wait("scalar", eV, vstep.get(st_ - 2, 0))
                vda = pl.op("scalar", lambda e, j=j, n=n, sp=sp: e.activation(
                    out=dA2[sp], in_=delta_f, func=AF.Exp,
                    scale=negA[:, (j * NST + n):(j * NST + n) + 1]), inc=eS)
                pl.wait("vector", eS, vda)
                pl.wait("vector", qb, v_bc[st_])
                pl.wait("vector", eT, vyadd.get(st_ - 4, 0))
                sp4 = st_ % 3
                pl.op("vector", lambda e, sp4=sp4: e.tensor_mul(
                    out=dBu_b, in0=du_b, in1=bcr[sp4][:, 0:C.CL]))
                col = j * NST + n
                if lc == 0:
                    pl.op("vector", lambda e, sp=sp: e.tensor_tensor_scan(
                        out=h_b, data0=dA2[sp], data1=dBu_b, initial=0.0,
                        op0=OP.mult, op1=OP.add))
                else:
                    pl.op("vector", lambda e, sp=sp, col=col: e.tensor_tensor_scan(
                        out=h_b, data0=dA2[sp], data1=dBu_b,
                        initial=carry[:, col:col + 1], op0=OP.mult, op1=OP.add))
                vstep[st_] = pl.op("vector", lambda e, spy=spy, sp4=sp4: e.tensor_mul(
                    out=ytmp[spy], in0=h_b, in1=bcr[sp4][:, C.CL:2 * C.CL]), inc=eV)
                if lc + 1 < C.NLC:
                    pl.op("vector", lambda e, col=col: e.tensor_copy(
                        out=carry[:, col:col + 1], in_=h_b[:, C.CL - 1:C.CL]))
                if DBG_TAPS and t == DBG_T and n == 0:
                    pl.wait("sync", eV, vstep[st_])
                    pl.dma("sync", scd_d[0], delta_f, qs)
                    pl.dma("sync", scd_d[1], dA2[sp], qs)
                    pl.dma("sync", scdB_d[0], dBu_b, qs)
                    pl.dma("sync", scdB_d[1], h_b, qs)
                    pl.dma("sync", car_d[:], carry[:], qs)
                    vdbg2 = pl.dma("sync", scdB_d[2], ytmp[sp], qs)
                    pl.wait("vector", qs, vdbg2)
                pl.wait("tensor", eV, vstep[st_])
                if n == 0:
                    pl.wait("tensor", eV, vgate.get(t - 1, 0))
                for ci in range(CBL):
                    vyadd[st_] = pl.op("tensor", lambda e, spy=spy, ci=ci, n=n: e.matmul(
                        out=pY[:, ci * C.CB:(ci + 1) * C.CB], lhsT=i128[:],
                        rhs=ytmp[spy][:, ci * C.CB:(ci + 1) * C.CB],
                        start=(n == 0), stop=False),
                        inc=eT if ci == CBL - 1 else None)
                emit_bcast(st_ + 3)
                sc_ = pl.cur_scope
                pump(1)
                pump(1)
                pl.set_scope(sc_)
            pl.wait("tensor", eV, vdg)
            vD = None
            for ci in range(CBL):
                vD = pl.op("tensor", lambda e, par=par, ci=ci: e.matmul(
                    out=pY[:, ci * C.CB:(ci + 1) * C.CB], lhsT=diagD[:],
                    rhs=u_sc[par][:, ci * C.CB:(ci + 1) * C.CB],
                    start=False, stop=True),
                    inc=eT if ci == CBL - 1 else None)
            vtD[t] = vD
            pl.wait("vector", eT, vD)
            pl.wait("vector", qa, vspU[t])
            rp = t % 2
            pl.wait("vector", qy, fstate["yslot"].get(t - 2, 0))
            vgate[t] = pl.op("vector", lambda e, par=par, rp=rp: e.tensor_mul(
                out=ybf[rp], in0=pY[:, 0:C.CL], in1=sz_sc[par]), inc=eV)
            pl.wait("scalar", eV, vgate[t])
            vys = pl.dma("scalar", y_d[j * 128:(j + 1) * 128, lo:lo + C.CL],
                         ybf[rp], qy)
            fstate["yslot"][t] = vys
            fstate["yspill"][lc] = vys
            emit_uld(t + 2)
            sc_ = pl.cur_scope
            pump(1)
            pl.set_scope(sc_)
        for ci in range(C.CL // C.CBM):
            genq.append(gen_m4(lc * (C.CL // C.CBM) + ci))

    while genq:
        try:
            next(genq[0])
        except StopIteration:
            genq.popleft()

    if debug:
        dbgs = [("w2sl_src", w2_sl.ap(), [C.NJ, 128, C.DMS], BF16),
                ("car_d", car_d, [128, C.NJ * C.NST], F32),
                ("scd_d", scd_d, [2, 128, C.CL], F32),
                ("scdB_d", scdB_d, [3, 128, C.CL], BF16),
                ("xnT_d", xnT_d, [C.DM, C.L], BF16),
                ("u_d", u_d, [C.DIN, C.L], BF16),
                ("sz_d", sz_d, [C.DIN, C.L], BF16),
                ("bc_d", bc_d, [2 * NST, C.L], BF16),
                ("y_d", y_d, [C.DIN, C.L], BF16),
                ("gT_d", gT_d, [C.DM, C.L], BF16),
                ("w2_d", w2_d, [C.NC * C.NJ, 128, C.DMS], BF16),
                ("wg_d", wg_d, [C.NC * C.OTP * C.NKD, 128, 128], BF16)]
        pl.wait("sync", qa, pl.emitted.get(qa, 0))
        pl.wait("sync", qs, pl.emitted.get(qs, 0))
        pl.wait("sync", qb, pl.emitted.get(qb, 0))
        pl.wait("sync", qc2, pl.emitted.get(qc2, 0))
        for nm, src, shp, dt in dbgs:
            dst = nc.dram_tensor(nm + "_dbg", shp, dt, kind="ExternalOutput").ap()
            pl.dma("sync", dst[:], src[:], qs)
    for q in (qa, qy, qs, qb, qc2, cc):
        pl.wait("sync", q, pl.emitted.get(q, 0))

    for name in sorted(pl.emitted.keys()):
        SEMS[name] = es.enter_context(nc.semaphore(name))
    def replay(e, eng):
        cur = None
        ctx = None
        for f, sc in zip(pl.ops[eng], pl.scopes[eng]):
            if sc != cur:
                if ctx is not None:
                    ctx.__exit__(None, None, None)
                ctx = nc.named_scope(f"{sc}_{eng[0]}")
                ctx.__enter__()
                cur = sc
            f(e)
        if ctx is not None:
            ctx.__exit__(None, None, None)

    with nc.Block() as block:
        @block.sync
        def _(e):
            replay(e, "sync")

        @block.vector
        def _(e):
            replay(e, "vector")

        @block.scalar
        def _(e):
            replay(e, "scalar")

        @block.tensor
        def _(e):
            replay(e, "tensor")

        @block.gpsimd
        def _(e):
            replay(e, "gpsimd")
    es.close()
    return nc


def _hadamard(n):
    H = np.array([[1.0]], dtype=np.float32)
    while H.shape[0] < n:
        H = np.block([[H, H], [H, -H]])
    return H


def host_inputs(cfg: Cfg, core, hidden, W_in, conv_w, conv_b, W_x, W_dt,
                dt_bias, A_log, D, W_out, lut, gate_idx, up_idx, down_idx):
    C = cfg
    bf = ml_dtypes.bfloat16
    f32 = np.float32
    H = (_hadamard(C.DIN) / math.sqrt(C.DIN)).astype(f32)
    Ht = np.ascontiguousarray(
        H.reshape(C.NJ, 128, C.NJ, 128).transpose(2, 0, 1, 3)).astype(bf)
    gi = np.zeros((C.DM, C.OTP * C.NC * 128), np.int32)
    gi[:, 0:C.INTER] = gate_idx.T
    ui = np.zeros((C.DM, C.OTP * C.NC * 128), np.int32)
    ui[:, 0:C.INTER] = up_idx.T
    di = np.ascontiguousarray(down_idx.T.astype(np.int32))
    o0 = core * C.OTP * 128
    m0 = core * C.DMS
    return {
        "hT": np.ascontiguousarray(hidden[core].T).astype(f32),
        "winT": np.ascontiguousarray(W_in.T).astype(f32),
        "wxT": np.ascontiguousarray(W_x.T).astype(f32),
        "wdtT": np.ascontiguousarray(W_dt.T).astype(f32),
        "wosl": np.ascontiguousarray(W_out.T[:, m0:m0 + C.DMS]).astype(f32),
        "convw": np.ascontiguousarray(
            conv_w.reshape(C.NJ, 128, C.KC).transpose(1, 0, 2)
            .reshape(128, C.NJ * C.KC)).astype(f32),
        "convb": np.ascontiguousarray(
            conv_b.reshape(C.NJ, 128).T).astype(f32),
        "dtb": np.ascontiguousarray(
            dt_bias.reshape(C.NJ, 128).T).astype(f32),
        "alog": np.ascontiguousarray(
            A_log.reshape(C.NJ, 128, C.NST).transpose(1, 0, 2)
            .reshape(128, C.NJ * C.NST)).astype(f32),
        "dvec": np.ascontiguousarray(
            D.reshape(C.NJ, 128).T).astype(f32),
        "lutv": np.ascontiguousarray(lut.reshape(1, -1)).astype(f32),
        "gidx": np.ascontiguousarray(gi[:, o0:o0 + C.OTP * 128]),
        "uidx": np.ascontiguousarray(ui[:, o0:o0 + C.OTP * 128]),
        "didx": np.ascontiguousarray(di[:, m0:m0 + C.DMS]),
        "Ht": Ht,
        "I128": np.eye(128, dtype=bf),
        "ones128": np.ones((128, 1), np.float32),
    }


def kernel(L0_attn_hidden_states, L0_attn_W_in, L0_attn_conv_w, L0_attn_conv_b,
           L0_attn_W_x, L0_attn_W_dt, L0_attn_dt_bias, L0_attn_A_log,
           L0_attn_D, L0_attn_W_out, L0_ffn_lut, L0_ffn_gate_idx,
           L0_ffn_up_idx, L0_ffn_down_idx):
    C = FULL
    args = (np.asarray(L0_attn_hidden_states), np.asarray(L0_attn_W_in),
            np.asarray(L0_attn_conv_w), np.asarray(L0_attn_conv_b),
            np.asarray(L0_attn_W_x), np.asarray(L0_attn_W_dt),
            np.asarray(L0_attn_dt_bias), np.asarray(L0_attn_A_log),
            np.asarray(L0_attn_D), np.asarray(L0_attn_W_out),
            np.asarray(L0_ffn_lut), np.asarray(L0_ffn_gate_idx),
            np.asarray(L0_ffn_up_idx), np.asarray(L0_ffn_down_idx))
    nc = build(C)
    in_maps = [host_inputs(C, c, *args) for c in range(C.NC)]
    res = run_bass_kernel_spmd(nc, in_maps, core_ids=list(range(C.NC)))
    out = np.stack([np.ascontiguousarray(res.results[c]["outT"].T)
                    for c in range(C.NC)], axis=0)
    return out.astype(np.float32)
